# version 03
# baseline (speedup 1.0000x reference)
import numpy as np
import ml_dtypes
import concourse.bass as bass
import concourse.mybir as mybir
from concourse.bass_utils import run_bass_kernel_spmd
from contextlib import ExitStack

F32 = mybir.dt.float32
BF16 = mybir.dt.bfloat16
I32 = mybir.dt.int32
I16 = mybir.dt.int16
AF = mybir.ActivationFunctionType
ALU = mybir.AluOpType

NCORES = 8
TOK = 1024
D = 2048
HC = 16
SEQ = 4096
NKV = 8
GELU_B = 0.044715
GELU_A = 0.7978845608028654

CROWS_G = 4 * 2048
CROWS_A = 5 * 512
OFF_WG, OFF_WU, OFF_WD, OFF_AT = 0, CROWS_G, 2 * CROWS_G, 3 * CROWS_G
CROWS = 3 * CROWS_G + CROWS_A


class Prog:
    ENGS = ("sync", "scalar", "vector", "tensor", "gpsimd")
    CENGS = ("scalar", "vector", "tensor")
    INC = {"scalar": 1, "vector": 1, "tensor": 1}

    def __init__(self, mksem):
        self.mksem = mksem
        self.ops = {e: [] for e in self.ENGS}
        self.cnt = {e: 0 for e in self.CENGS}
        self.W = {e: {} for e in self.ENGS}
        self.snap = {}
        self.state = {}
        self.ctr = {e: mksem("ctr_" + e) for e in self.CENGS}
        self.psem = {}
        self.stream = {}
        self.n_waits = 0

    def _deps(self, reads, writes):
        deps = {}
        for k in reads:
            st = self.state.get(k)
            if st is not None and st[0] is not None:
                src, idx = st[0]
                if deps.get(src, 0) < idx:
                    deps[src] = idx
        for k in writes:
            st = self.state.get(k)
            if st is not None:
                if st[0] is not None:
                    src, idx = st[0]
                    if deps.get(src, 0) < idx:
                        deps[src] = idx
                for src, idx in st[1].items():
                    if deps.get(src, 0) < idx:
                        deps[src] = idx
        return deps

    def _emit_waits(self, eng, deps):
        Wm = self.W[eng]
        for src in sorted(deps):
            idx = deps[src]
            if src == "tensor" and eng == "tensor":
                continue
            if src in self.stream:
                idx = self.stream[src][1]
            if Wm.get(src, 0) >= idx:
                continue
            if src in self.psem:
                self.ops[eng].append(
                    lambda e, h=self.psem[src]: e.wait_ge(h, 1))
            elif src in self.stream:
                v = idx * 16
                self.ops[eng].append(
                    lambda e, h=self.stream[src][0], v=v: e.wait_ge(h, v))
            else:
                v = idx * self.INC[src]
                self.ops[eng].append(
                    lambda e, h=self.ctr[src], v=v: e.wait_ge(h, v))
            self.n_waits += 1
            sn = self.snap.get((src, idx))
            if sn:
                for s2, i2 in sn.items():
                    if Wm.get(s2, 0) < i2:
                        Wm[s2] = i2
            if Wm.get(src, 0) < idx:
                Wm[src] = idx

    def _post(self, name, eng, myidx, reads, writes):
        for k in reads:
            st = self.state.get(k)
            if st is None:
                st = self.state[k] = [None, {}]
            st[1][name] = myidx
        for k in writes:
            self.state[k] = [(name, myidx), {}]
        self.snap[(name, myidx)] = dict(self.W[eng])

    def ins(self, eng, fn, reads=(), writes=(), stream=None):
        deps = self._deps(reads, writes)
        self._emit_waits(eng, deps)
        if eng in self.CENGS:
            self.cnt[eng] += 1
            myidx = self.cnt[eng]
            h = self.ctr[eng]
            self.ops[eng].append(lambda e, f=fn, h=h: f(e).then_inc(h, 1))
            self._post(eng, eng, myidx, reads, writes)
        else:
            sname = stream if stream is not None else writes[0][0]
            st = self.stream.get(sname)
            if st is None:
                st = self.stream[sname] = [self.mksem("dma_" + sname), 0]
            st[1] += 1
            myidx = st[1]
            self.ops[eng].append(
                lambda e, f=fn, h=st[0]: f(e).then_inc(h, 16))
            self._post(sname, eng, myidx, reads, writes)

    def col(self, psname, fn, reads=(), writes=()):
        deps = self._deps(reads, writes)
        self._emit_waits("gpsimd", deps)
        h = self.psem[psname]
        self.ops["gpsimd"].append(lambda e, f=fn, h=h: f(e).then_inc(h, 1))
        self._post(psname, "gpsimd", 1, reads, writes)

    def barrier(self):
        targets = [(f, self.cnt[f], self.ctr[f], self.INC[f])
                   for f in self.CENGS if self.cnt[f] > 0]
        targets += [(s, st[1], st[0], 16)
                    for s, st in self.stream.items() if st[1] > 0]
        for e in self.ENGS:
            Wm = self.W[e]
            for f, n, h, inc in targets:
                if f == e or Wm.get(f, 0) >= n:
                    continue
                v = n * inc
                self.ops[e].append(lambda en, h=h, v=v: en.wait_ge(h, v))
                Wm[f] = n
                self.n_waits += 1

    def final_wait(self, eng, sname):
        st = self.stream[sname]
        v = st[1] * 16
        self.ops[eng].append(lambda e, h=st[0], v=v: e.wait_ge(h, v))

    def run(self, eng_name, e):
        for f in self.ops[eng_name]:
            f(e)


def build_program():
    nc = bass.Bass("TRN2", target_bir_lowering=False)

    x_in = nc.dram_tensor("x_in", [TOK, D], F32, kind="ExternalInput")
    gp = nc.dram_tensor("gp", [2048, 1024], I32, kind="ExternalInput")
    gsc = nc.dram_tensor("gsc", [2048, 64], F32, kind="ExternalInput")
    up_p = nc.dram_tensor("up_p", [2048, 1024], I32, kind="ExternalInput")
    usc = nc.dram_tensor("usc", [2048, 64], F32, kind="ExternalInput")
    dp = nc.dram_tensor("dp", [2048, 1024], I32, kind="ExternalInput")
    dsc = nc.dram_tensor("dsc", [2048, 64], F32, kind="ExternalInput")
    aw = nc.dram_tensor("aw", [640, D], F32, kind="ExternalInput")
    cosq = nc.dram_tensor("cosq", [128, TOK], F32, kind="ExternalInput")
    sinq = nc.dram_tensor("sinq", [128, TOK], F32, kind="ExternalInput")
    out = nc.dram_tensor("out", [TOK, D], F32, kind="ExternalOutput")

    contrib = nc.dram_tensor("contrib", [CROWS, 512], BF16)
    g_w = nc.dram_tensor("g_w", [NCORES * CROWS, 512], BF16)
    k_c = nc.dram_tensor("k_c", [512, TOK], BF16)
    vp_c = nc.dram_tensor("vp_c", [128, NKV * 8 * 65], BF16)
    g_k = nc.dram_tensor("g_k", [4 * 512, TOK], BF16)
    g_vp = nc.dram_tensor("g_vp", [4 * 128, NKV * 8 * 65], BF16)

    def gw_off(mat, r, b):
        return (mat * NCORES * CROWS_G + r * CROWS_G + b * 2048) * 512

    def gwd_base(r):
        return (2 * NCORES * CROWS_G + r * CROWS_G) * 512

    def gatt_off(i):
        r, j = i // 5, i % 5
        return (3 * NCORES * CROWS_G + r * CROWS_A + j * 512) * 512

    identF = nc.inline_tensor(np.eye(128, dtype=np.float32), "identF")
    identB = nc.inline_tensor(np.eye(128).astype(ml_dtypes.bfloat16), "identB")
    ropeT_np = np.zeros((128, 128), np.float32)
    for b in range(2):
        for i in range(32):
            ropeT_np[b * 64 + i + 32, b * 64 + i] = -1.0
            ropeT_np[b * 64 + i, b * 64 + i + 32] = 1.0
    ropeT = nc.inline_tensor(ropeT_np, "ropeT")
    ones128 = nc.inline_tensor(np.ones((128, 1), np.float32), "ones128")
    bc128 = nc.inline_tensor(np.ones((1, 128), np.float32), "bc128")
    bc64 = nc.inline_tensor(np.ones((1, 64), np.float32), "bc64")

    ctx = ExitStack()
    sb = lambda n, s, d: ctx.enter_context(nc.sbuf_tensor(n, s, d))
    ps = lambda n, s, d: ctx.enter_context(nc.psum_tensor(n, s, d))
    mksem = lambda n: ctx.enter_context(nc.semaphore(n))

    P = Prog(mksem)
    for a in range(6):
        P.psem["ag%d" % a] = mksem("ag%d" % a)

    pA = ps("pA", [128, 1024], F32)
    pB = ps("pB", [128, 1024], F32)
    pC = ps("pC", [128, 1024], F32)
    pEa = ps("pEa", [128, 512], BF16)
    pEb = ps("pEb", [128, 512], BF16)
    pE = [pEa, pEb]
    pEk = ["pEa", "pEb"]
    A01 = (("pA", 0), ("pA", 1))
    B01 = (("pB", 0), ("pB", 1))
    C01 = (("pC", 0), ("pC", 1))

    x_sb = sb("x_sb", [128, HC * 1024], F32)
    c_id_f = sb("c_id_f", [128, 128], F32)
    c_id_b = sb("c_id_b", [128, 128], BF16)
    c_rope = sb("c_rope", [128, 128], F32)
    c_ones = sb("c_ones", [128, 1], F32)
    c_bc128 = sb("c_bc128", [1, 128], F32)
    c_bc64 = sb("c_bc64", [1, 64], F32)
    c_cos = sb("c_cos", [128, TOK], F32)
    c_sin = sb("c_sin", [128, TOK], F32)

    for dst, src, key in [
            (c_id_f, identF, "c_id_f"), (c_id_b, identB, "c_id_b"),
            (c_rope, ropeT, "c_rope"), (c_ones, ones128, "c_ones"),
            (c_bc128, bc128, "c_bc128"), (c_bc64, bc64, "c_bc64"),
            (c_cos, cosq, "c_cos"), (c_sin, sinq, "c_sin")]:
        P.ins("sync", lambda e, d=dst, s=src: e.dma_start(d[:], s.ap()),
              writes=[(key,)], stream="consts")

    def emit_ln(xn_sb, xnk, lnS, lnk):
        st_sum, st_sq, st_rstd, st_mrs, bc_rstd, bc_mrs, ln_tmp = lnS
        (k_sum, k_sq, k_rstd, k_mrs, k_brstd, k_bmrs, k_tmp) = lnk
        for th in range(2):
            for hc in range(HC):
                P.ins("tensor", lambda e, hc=hc, th=th: e.matmul(
                    pA[0:1, th * 512:(th + 1) * 512], c_ones[:],
                    x_sb[:, hc * 1024 + th * 512: hc * 1024 + (th + 1) * 512],
                    start=(hc == 0), stop=(hc == HC - 1), skip_group_check=True),
                    reads=[("c_ones",), ("x_sb", hc), ("pA", th)],
                    writes=[("pA", th)])
        for hc in range(HC):
            P.ins("scalar", lambda e, hc=hc: e.activation(
                ln_tmp[:], x_sb[:, hc * 1024:(hc + 1) * 1024], AF.Square),
                reads=[("x_sb", hc)], writes=[(k_tmp,)])
            for th in range(2):
                P.ins("tensor", lambda e, hc=hc, th=th: e.matmul(
                    pB[0:1, th * 512:(th + 1) * 512], c_ones[:],
                    ln_tmp[:, th * 512:(th + 1) * 512],
                    start=(hc == 0), stop=(hc == HC - 1), skip_group_check=True),
                    reads=[("c_ones",), (k_tmp,), ("pB", th)],
                    writes=[("pB", th)])
        P.ins("vector", lambda e: e.tensor_single_scalar(
            st_sum[:], pA[0:1, :], 1.0 / D, ALU.mult),
            reads=list(A01), writes=[(k_sum,)])
        P.ins("vector", lambda e: e.tensor_single_scalar(
            st_sq[:], pB[0:1, :], 1.0 / D, ALU.mult),
            reads=list(B01), writes=[(k_sq,)])
        P.ins("vector", lambda e: e.tensor_mul(st_rstd[:], st_sum[:], st_sum[:]),
              reads=[(k_sum,)], writes=[(k_rstd,)])
        P.ins("vector", lambda e: e.tensor_sub(st_rstd[:], st_sq[:], st_rstd[:]),
              reads=[(k_sq,), (k_rstd,)], writes=[(k_rstd,)])
        P.ins("scalar", lambda e: e.activation(st_rstd[:], st_rstd[:], AF.Sqrt,
                                               bias=np.float32(1e-5)),
              reads=[(k_rstd,)], writes=[(k_rstd,)])
        P.ins("vector", lambda e: e.reciprocal(st_rstd[:], st_rstd[:]),
              reads=[(k_rstd,)], writes=[(k_rstd,)])
        P.ins("vector", lambda e: e.tensor_mul(st_mrs[:], st_sum[:], st_rstd[:]),
              reads=[(k_sum,), (k_rstd,)], writes=[(k_mrs,)])
        for th in range(2):
            P.ins("tensor", lambda e, th=th: e.matmul(
                pA[:, th * 512:(th + 1) * 512], c_bc128[:],
                st_rstd[0:1, th * 512:(th + 1) * 512], start=True, stop=True,
                skip_group_check=True),
                reads=[("c_bc128",), (k_rstd,)], writes=[("pA", th)])
            P.ins("tensor", lambda e, th=th: e.matmul(
                pB[:, th * 512:(th + 1) * 512], c_bc128[:],
                st_mrs[0:1, th * 512:(th + 1) * 512], start=True, stop=True,
                skip_group_check=True),
                reads=[("c_bc128",), (k_mrs,)], writes=[("pB", th)])
        P.ins("scalar", lambda e: e.copy(bc_rstd[:], pA[:, :]),
              reads=list(A01), writes=[(k_brstd,)])
        P.ins("scalar", lambda e: e.copy(bc_mrs[:], pB[:, :]),
              reads=list(B01), writes=[(k_bmrs,)])
        for hc in range(HC):
            P.ins("vector", lambda e, hc=hc: e.tensor_tensor(
                ln_tmp[:], x_sb[:, hc * 1024:(hc + 1) * 1024], bc_rstd[:],
                ALU.mult),
                reads=[("x_sb", hc), (k_brstd,)], writes=[(k_tmp,)])
            P.ins("vector", lambda e, hc=hc, xn_sb=xn_sb: e.tensor_tensor(
                xn_sb[:, hc * 1024:(hc + 1) * 1024], ln_tmp[:], bc_mrs[:],
                ALU.subtract),
                reads=[(k_tmp,), (k_bmrs,)], writes=[(xnk, hc)])

    def emit_ffn(xn_sb, xnk, wg_b, wu_b, wd_b, inter, gl, keys):
        gl_g, gl_a, gl_b, gl_c = gl
        kwg, kwu, kwd, kint, kg, ka, kb_, kc = keys
        for s in range(32):
            r, b = s // 4, s % 4
            P.ins("sync", lambda e, r=r, b=b: e.dma_start(
                wg_b[:], bass.AP(g_w, gw_off(0, r, b), [[8192, 128], [1, 8192]])),
                reads=[("g_w", "g")], writes=[(kwg,)])
            P.ins("sync", lambda e, r=r, b=b: e.dma_start(
                wu_b[:], bass.AP(g_w, gw_off(1, r, b), [[8192, 128], [1, 8192]])),
                reads=[("g_w", "u")], writes=[(kwu,)])
            P.ins("sync", lambda e, r=r, b=b: e.dma_start(
                wd_b[:].rearrange("p (cc n) -> p cc n", cc=16),
                bass.AP(g_w, gwd_base(r) + 512 * b,
                        [[2048, 128], [262144, 16], [1, 512]])),
                reads=[("g_w", "d")], writes=[(kwd,)])
            for nn in range(4):
                for th in range(2):
                    for hcc in range(HC):
                        P.ins("tensor", lambda e, nn=nn, th=th, hcc=hcc: e.matmul(
                            pA[:, th * 512:(th + 1) * 512],
                            wg_b[:, hcc * 512 + nn * 128: hcc * 512 + nn * 128 + 128],
                            xn_sb[:, hcc * 1024 + th * 512: hcc * 1024 + (th + 1) * 512],
                            start=(hcc == 0), stop=(hcc == HC - 1),
                            skip_group_check=True),
                            reads=[(kwg,), (xnk, hcc), ("pA", th)],
                            writes=[("pA", th)])
                for th in range(2):
                    for hcc in range(HC):
                        P.ins("tensor", lambda e, nn=nn, th=th, hcc=hcc: e.matmul(
                            pB[:, th * 512:(th + 1) * 512],
                            wu_b[:, hcc * 512 + nn * 128: hcc * 512 + nn * 128 + 128],
                            xn_sb[:, hcc * 1024 + th * 512: hcc * 1024 + (th + 1) * 512],
                            start=(hcc == 0), stop=(hcc == HC - 1),
                            skip_group_check=True),
                            reads=[(kwu,), (xnk, hcc), ("pB", th)],
                            writes=[("pB", th)])
                P.ins("scalar", lambda e: e.copy(gl_g[:], pA[:, :]),
                      reads=list(A01), writes=[(kg,)])
                P.ins("scalar", lambda e: e.activation(gl_a[:], gl_g[:], AF.Square),
                      reads=[(kg,)], writes=[(ka,)])
                P.ins("vector", lambda e: e.tensor_scalar(
                    gl_b[:], gl_a[:], GELU_B, 1.0, ALU.mult, ALU.add),
                    reads=[(ka,)], writes=[(kb_,)])
                P.ins("vector", lambda e: e.tensor_mul(gl_a[:], gl_g[:], gl_b[:]),
                      reads=[(kg,), (kb_,)], writes=[(ka,)])
                P.ins("scalar", lambda e: e.activation(
                    gl_b[:], gl_a[:], AF.Tanh, scale=np.float32(GELU_A)),
                    reads=[(ka,)], writes=[(kb_,)])
                P.ins("scalar", lambda e: e.copy(gl_c[:], pB[:, :]),
                      reads=list(B01), writes=[(kc,)])
                P.ins("vector", lambda e: e.tensor_mul(gl_a[:], gl_g[:], gl_b[:]),
                      reads=[(kg,), (kb_,)], writes=[(ka,)])
                P.ins("vector", lambda e: e.tensor_add(gl_b[:], gl_a[:], gl_g[:]),
                      reads=[(ka,), (kg,)], writes=[(kb_,)])
                P.ins("vector", lambda e, nn=nn: e.tensor_mul(
                    inter[:, nn * 1024:(nn + 1) * 1024], gl_b[:], gl_c[:]),
                    reads=[(kb_,), (kc,)], writes=[(kint, nn)])
            for hcc in range(HC):
                for th in range(2):
                    for nn in range(4):
                        P.ins("tensor", lambda e, hcc=hcc, th=th, nn=nn: e.matmul(
                            pC[:, th * 512:(th + 1) * 512],
                            wd_b[:, hcc * 512 + nn * 128: hcc * 512 + nn * 128 + 128],
                            inter[:, nn * 1024 + th * 512: nn * 1024 + (th + 1) * 512],
                            start=(nn == 0), stop=(nn == 3), skip_group_check=True),
                            reads=[(kwd,), (kint, nn), ("pC", th)],
                            writes=[("pC", th)])
                P.ins("vector", lambda e, hcc=hcc: e.scalar_tensor_tensor(
                    x_sb[:, hcc * 1024:(hcc + 1) * 1024], pC[:, :], 0.25,
                    x_sb[:, hcc * 1024:(hcc + 1) * 1024], ALU.mult, ALU.add),
                    reads=list(C01) + [("x_sb", hcc)], writes=[("x_sb", hcc)])

    ctxA = ExitStack()
    sbA = lambda n, s, d: ctxA.enter_context(nc.sbuf_tensor(n, s, d))
    x_stage = sbA("x_stage", [128, 8 * D], F32)

    P.ins("sync", lambda e: e.dma_start(
        x_stage[:].rearrange("p (tc h) -> p tc h", tc=8),
        x_in.ap().rearrange("(tc p) h -> p tc h", p=128)),
        writes=[("x_stage",)])
    n_tr = [0]
    for hc in range(HC):
        for tc in range(8):
            pT, pk = (pA, "pA") if n_tr[0] % 2 == 0 else (pB, "pB")
            P.ins("tensor", lambda e, hc=hc, tc=tc, pT=pT: e.transpose(
                pT[:, 0:128],
                x_stage[:, tc * D + hc * 128: tc * D + hc * 128 + 128],
                c_id_f[:]),
                reads=[("x_stage",), ("c_id_f",)], writes=[(pk, 0)])
            P.ins("vector", lambda e, hc=hc, tc=tc, pT=pT: e.tensor_copy(
                x_sb[:, hc * 1024 + tc * 128: hc * 1024 + tc * 128 + 128],
                pT[:, 0:128]),
                reads=[(pk, 0)], writes=[("x_sb", hc)])
            n_tr[0] += 1

    dq_pk32 = sbA("dq_pk32", [128, 1024], I32)
    dq_p16 = sbA("dq_p16", [128, 1024], I16)
    dq_m = sbA("dq_m", [128, 1024], I16)
    dq_sg = sbA("dq_sg", [128, 1024], I16)
    dq_u = sbA("dq_u", [128, 1024], BF16)
    dq_r = sbA("dq_r", [128, 1024], BF16)
    dq_w = sbA("dq_w", [128, 1024], BF16)
    dq_w2 = sbA("dq_w2", [128, 1024], BF16)
    dq_sc = sbA("dq_sc", [128, 64], F32)
    wnat = sbA("wnat", [128, 2048], BF16)
    c_sb = sbA("c_sb", [128, HC * 512], BF16)
    catt = sbA("catt", [128, HC * 128], BF16)
    aw_sb = sbA("aw_sb", [128, D], F32)
    aw_b = sbA("aw_b", [128, D], BF16)
    sc_bcast = bass.AP(dq_sc, 0, [[64, 128], [1, 64], [0, 16]])

    n_trw = [0]

    def emit_dq_chunk(pk_t, sc_t, cc):
        P.ins("gpsimd", lambda e, cc=cc, pk_t=pk_t: e.dma_start(
            dq_pk32[:], pk_t.ap()[cc * 128:(cc + 1) * 128, :]),
            writes=[("dq_pk32",)])
        P.ins("gpsimd", lambda e, cc=cc, sc_t=sc_t: e.dma_start(
            dq_sc[:], sc_t.ap()[cc * 128:(cc + 1) * 128, :]),
            writes=[("dq_sc",)])
        P.ins("vector", lambda e: e.tensor_copy(dq_p16[:], dq_pk32[:]),
              reads=[("dq_pk32",)], writes=[("dq_p16",)])
        for half in range(2):
            if half == 0:
                P.ins("vector", lambda e: e.tensor_scalar(
                    dq_m[:], dq_p16[:], 7, 6,
                    ALU.bitwise_and, ALU.logical_shift_left),
                    reads=[("dq_p16",)], writes=[("dq_m",)])
                P.ins("vector", lambda e: e.tensor_scalar(
                    dq_sg[:], dq_p16[:], 8, 12,
                    ALU.bitwise_and, ALU.logical_shift_left),
                    reads=[("dq_p16",)], writes=[("dq_sg",)])
            else:
                P.ins("vector", lambda e: e.tensor_scalar(
                    dq_m[:], dq_p16[:], 2, 0x1C0,
                    ALU.logical_shift_left, ALU.bitwise_and),
                    reads=[("dq_p16",)], writes=[("dq_m",)])
                P.ins("vector", lambda e: e.tensor_scalar(
                    dq_sg[:], dq_p16[:], 128, 8,
                    ALU.bitwise_and, ALU.logical_shift_left),
                    reads=[("dq_p16",)], writes=[("dq_sg",)])
            P.ins("vector", lambda e: e.tensor_single_scalar(
                dq_u[:].bitcast(I16), dq_m[:], 0x3F00, ALU.add),
                reads=[("dq_m",)], writes=[("dq_u",)])
            P.ins("scalar", lambda e: e.activation(
                dq_r[:], dq_u[:], AF.Relu, bias=np.float32(1.0),
                scale=np.float32(-1.0)),
                reads=[("dq_u",)], writes=[("dq_r",)])
            wt, wk = (dq_w, "dq_w") if half == 0 else (dq_w2, "dq_w2")
            P.ins("vector", lambda e, wt=wt: e.tensor_sub(wt[:], dq_u[:], dq_r[:]),
                  reads=[("dq_u",), ("dq_r",)], writes=[(wk,)])
            P.ins("vector", lambda e, wt=wt: e.tensor_add(
                wt[:].bitcast(I16), wt[:].bitcast(I16), dq_sg[:]),
                reads=[(wk,), ("dq_sg",)], writes=[(wk,)])
        P.ins("vector", lambda e: e.tensor_tensor(
            wnat[:, 0:2048:2], dq_w[:], sc_bcast, ALU.mult),
            reads=[("dq_w",), ("dq_sc",)], writes=[("wnat",)])
        P.ins("vector", lambda e: e.tensor_tensor(
            wnat[:, 1:2048:2], dq_w2[:], sc_bcast, ALU.mult),
            reads=[("dq_w2",), ("dq_sc",)], writes=[("wnat",)])

    def emit_transposes(src_sb, src_key, dst_fn, dst_key):
        for hh in range(16):
            pT, pk = pE[n_trw[0] % 2], pEk[n_trw[0] % 2]
            P.ins("tensor", lambda e, hh=hh, pT=pT, s=src_sb: e.transpose(
                pT[:, 0:128], s[:, hh * 128:(hh + 1) * 128], c_id_b[:]),
                reads=[(src_key,), ("c_id_b",)], writes=[(pk,)])
            P.ins("vector", lambda e, hh=hh, pT=pT, f=dst_fn: e.tensor_copy(
                f(hh), pT[:, 0:128]),
                reads=[(pk,)], writes=[(dst_key,)])
            n_trw[0] += 1

    for mat, (pk_t, sc_t, off, gk) in enumerate(
            [(gp, gsc, OFF_WG, "g"), (up_p, usc, OFF_WU, "u")]):
        for cc in range(16):
            emit_dq_chunk(pk_t, sc_t, cc)
            emit_transposes(wnat, "wnat",
                            lambda hh, cc=cc: c_sb[:, hh * 512 + (cc % 4) * 128:
                                                   hh * 512 + (cc % 4) * 128 + 128],
                            "c_sb")
            if cc % 4 == 3:
                b = cc // 4
                P.ins("gpsimd", lambda e, off=off, b=b: e.dma_start(
                    bass.AP(contrib, (off + b * 2048) * 512,
                            [[8192, 128], [1, 8192]]),
                    c_sb[:]),
                    reads=[("c_sb",)], writes=[("contrib", gk, b)])
        P.col("ag%d" % mat, lambda e, off=off, mat=mat: e.collective_compute(
            "AllGather", ALU.bypass, replica_groups=[list(range(8))],
            ins=[contrib.ap()[off:off + CROWS_G, :].opt()],
            outs=[bass.AP(g_w, mat * NCORES * CROWS_G * 512,
                          [[512, NCORES * CROWS_G], [1, 512]])],
        ), reads=[("contrib", gk, b) for b in range(4)], writes=[("g_w", gk)])

    for cc in range(16):
        emit_dq_chunk(dp, dsc, cc)
        emit_transposes(wnat, "wnat",
                        lambda hh: c_sb[:, hh * 128:(hh + 1) * 128], "c_sb")
        P.ins("gpsimd", lambda e, cc=cc: e.dma_start(
            bass.AP(contrib, (OFF_WD + cc * 512) * 512, [[2048, 128], [1, 2048]]),
            c_sb[:, 0:2048]),
            reads=[("c_sb",)], writes=[("contrib", "d", cc)])
    P.col("ag2", lambda e: e.collective_compute(
        "AllGather", ALU.bypass, replica_groups=[list(range(8))],
        ins=[contrib.ap()[OFF_WD:OFF_WD + CROWS_G, :].opt()],
        outs=[bass.AP(g_w, 2 * NCORES * CROWS_G * 512,
                      [[512, NCORES * CROWS_G], [1, 512]])],
    ), reads=[("contrib", "d", cc) for cc in range(16)], writes=[("g_w", "d")])

    P.ins("sync", lambda e: e.dma_start(aw_sb[:], aw.ap()[0:128, :]),
          writes=[("aw_sb",)])
    for j in range(5):
        P.ins("vector", lambda e: e.tensor_copy(aw_b[:], aw_sb[:]),
              reads=[("aw_sb",)], writes=[("aw_b",)])
        if j < 4:
            P.ins("sync", lambda e, j=j: e.dma_start(
                aw_sb[:], aw.ap()[(j + 1) * 128:(j + 2) * 128, :]),
                writes=[("aw_sb",)])
        emit_transposes(aw_b, "aw_b",
                        lambda hh: catt[:, hh * 128:(hh + 1) * 128], "catt")
        P.ins("gpsimd", lambda e, j=j: e.dma_start(
            bass.AP(contrib, (OFF_AT + j * 512) * 512, [[2048, 128], [1, 2048]]),
            catt[:]),
            reads=[("catt",)], writes=[("contrib", "a", j)])
    P.col("ag3", lambda e: e.collective_compute(
        "AllGather", ALU.bypass, replica_groups=[list(range(8))],
        ins=[contrib.ap()[OFF_AT:OFF_AT + CROWS_A, :].opt()],
        outs=[bass.AP(g_w, 3 * NCORES * CROWS_G * 512,
                      [[512, NCORES * CROWS_A], [1, 512]])],
    ), reads=[("contrib", "a", j) for j in range(5)], writes=[("g_w", "a")])

    P.barrier()
    ctxA.close()

    ctxB = ExitStack()
    sbB = lambda n, s, d: ctxB.enter_context(nc.sbuf_tensor(n, s, d))
    lnB = (sbB("st_sumB", [1, 1024], F32), sbB("st_sqB", [1, 1024], F32),
           sbB("st_rstdB", [1, 1024], F32), sbB("st_mrsB", [1, 1024], F32),
           sbB("bc_rstdB", [128, 1024], F32), sbB("bc_mrsB", [128, 1024], F32),
           sbB("ln_tmpB", [128, 1024], F32))
    lnBk = ("st_sumB", "st_sqB", "st_rstdB", "st_mrsB",
            "bc_rstdB", "bc_mrsB", "ln_tmpB")
    xn1 = sbB("xn1", [128, HC * 1024], BF16)
    wg1 = sbB("wg1", [128, HC * 512], BF16)
    wu1 = sbB("wu1", [128, HC * 512], BF16)
    wd1 = sbB("wd1", [128, HC * 512], BF16)
    int1 = sbB("int1", [128, 4 * 1024], BF16)
    gl1 = (sbB("glg1", [128, 1024], F32), sbB("gla1", [128, 1024], F32),
           sbB("glb1", [128, 1024], F32), sbB("glc1", [128, 1024], F32))

    emit_ln(xn1, "xn1", lnB, lnBk)
    emit_ffn(xn1, "xn1", wg1, wu1, wd1, int1, gl1,
             ("wg1", "wu1", "wd1", "int1", "glg1", "gla1", "glb1", "glc1"))
    P.barrier()
    ctxB.close()

    ctxC = ExitStack()
    sbC = lambda n, s, d: ctxC.enter_context(nc.sbuf_tensor(n, s, d))
    lnC = (sbC("st_sumC", [1, 1024], F32), sbC("st_sqC", [1, 1024], F32),
           sbC("st_rstdC", [1, 1024], F32), sbC("st_mrsC", [1, 1024], F32),
           sbC("bc_rstdC", [128, 1024], F32), sbC("bc_mrsC", [128, 1024], F32),
           sbC("ln_tmpC", [128, 1024], F32))
    lnCk = ("st_sumC", "st_sqC", "st_rstdC", "st_mrsC",
            "bc_rstdC", "bc_mrsC", "ln_tmpC")
    xn_att = sbC("xn_att", [128, HC * 1024], BF16)
    q_sb = sbC("q_sb", [128, HC * 1024], BF16)
    wq_b = sbC("wq_b", [128, HC * 128], BF16)
    wq_b2 = sbC("wq_b2", [128, HC * 128], BF16)
    rope_f, k_ropef = lnC[6], "ln_tmpC"
    rope_g, k_ropeg = lnC[4], "bc_rstdC"

    emit_ln(xn_att, "xn_att", lnC, lnCk)

    ctxC1 = ExitStack()
    sbC1 = lambda n, s, d: ctxC1.enter_context(nc.sbuf_tensor(n, s, d))
    kst = sbC1("kst", [128, 4 * 1024], BF16)
    vst = sbC1("vst", [128, NKV * 8 * 65], BF16)
    wv_sb = sbC1("wv_sb", [128, HC * 512], BF16)

    ones_ap = bass.AP(vst, 64, [[NKV * 8 * 65, 128], [520, NKV], [65, 8], [1, 1]])
    P.ins("vector", lambda e: e.memset(ones_ap, 1.0),
          writes=[("vst", tc) for tc in range(8)])

    for ec4 in range(4):
        P.ins("sync", lambda e, ec4=ec4: e.dma_start(
            bass.AP(wv_sb, ec4 * 128, [[HC * 512, 128], [512, 16], [1, 128]]),
            bass.AP(g_w, gatt_off(20 + ec4), [[2048, 128], [128, 16], [1, 128]])),
            reads=[("g_w", "a")], writes=[("wv_sb",)])

    def load_att_chunk(buf, bk, i):
        P.ins("sync", lambda e, buf=buf, i=i: e.dma_start(
            buf[:], bass.AP(g_w, gatt_off(i), [[2048, 128], [1, 2048]])),
            reads=[("g_w", "a")], writes=[(bk,)])

    qk_list = [(i, q_sb, "q_sb", i) for i in range(16)] + \
              [(16 + i, kst, "kst", i) for i in range(4)]
    cur, ck = wq_b, "wq_b"
    nxt, nk = wq_b2, "wq_b2"
    load_att_chunk(cur, ck, qk_list[0][0])
    for idx, (ci, dsb, dk, dj) in enumerate(qk_list):
        if idx + 1 < len(qk_list):
            load_att_chunk(nxt, nk, qk_list[idx + 1][0])
        for th in range(2):
            for hcc in range(HC):
                P.ins("tensor", lambda e, th=th, hcc=hcc, cur=cur: e.matmul(
                    pA[:, th * 512:(th + 1) * 512],
                    cur[:, hcc * 128:(hcc + 1) * 128],
                    xn_att[:, hcc * 1024 + th * 512: hcc * 1024 + (th + 1) * 512],
                    start=(hcc == 0), stop=(hcc == HC - 1), skip_group_check=True),
                    reads=[(ck,), ("xn_att", hcc), ("pA", th)],
                    writes=[("pA", th)])
        P.ins("scalar", lambda e: e.copy(rope_f[:], pA[:, :]),
              reads=list(A01), writes=[(k_ropef,)])
        for th in range(2):
            P.ins("tensor", lambda e, th=th: e.matmul(
                pB[:, th * 512:(th + 1) * 512], c_rope[:],
                rope_f[:, th * 512:(th + 1) * 512], start=True, stop=True,
                skip_group_check=True),
                reads=[("c_rope",), (k_ropef,)], writes=[("pB", th)])
        P.ins("vector", lambda e: e.tensor_mul(rope_g[:], rope_f[:], c_cos[:]),
              reads=[(k_ropef,), ("c_cos",)], writes=[(k_ropeg,)])
        P.ins("vector", lambda e: e.tensor_mul(rope_f[:], pB[:, :], c_sin[:]),
              reads=list(B01) + [("c_sin",)], writes=[(k_ropef,)])
        P.ins("vector", lambda e, dsb=dsb, dj=dj: e.tensor_add(
            dsb[:, dj * 1024:dj * 1024 + 1024], rope_g[:], rope_f[:]),
            reads=[(k_ropeg,), (k_ropef,)], writes=[(dk, dj)])
        cur, ck, nxt, nk = nxt, nk, cur, ck

    for tc in range(8):
        for hcc in range(HC):
            for ec4 in range(4):
                P.ins("tensor", lambda e, tc=tc, hcc=hcc, ec4=ec4: e.matmul(
                    pA[:, ec4 * 128:(ec4 + 1) * 128],
                    xn_att[:, hcc * 1024 + tc * 128: hcc * 1024 + tc * 128 + 128],
                    wv_sb[:, hcc * 512 + ec4 * 128: hcc * 512 + (ec4 + 1) * 128],
                    start=(hcc == 0), stop=(hcc == HC - 1), skip_group_check=True),
                    reads=[("xn_att", hcc), ("wv_sb",), ("pA", 0)],
                    writes=[("pA", 0)])
        for kv in range(NKV):
            P.ins("vector", lambda e, tc=tc, kv=kv: e.tensor_copy(
                vst[:, kv * 520 + tc * 65: kv * 520 + tc * 65 + 64],
                pA[:, kv * 64:(kv + 1) * 64]),
                reads=[("pA", 0)], writes=[("vst", tc)])

    P.ins("gpsimd", lambda e: e.dma_start(
        k_c.ap().rearrange("(ec p) t -> p ec t", p=128),
        kst[:].rearrange("p (ec t) -> p ec t", ec=4)),
        reads=[("kst", i) for i in range(4)], writes=[("k_c",)])
    P.ins("gpsimd", lambda e: e.dma_start(vp_c.ap(), vst[:]),
          reads=[("vst", tc) for tc in range(8)], writes=[("vp_c",)])
    P.col("ag4", lambda e: e.collective_compute(
        "AllGather", ALU.bypass, replica_groups=[[0, 1, 2, 3], [4, 5, 6, 7]],
        ins=[k_c.ap().opt()], outs=[g_k.ap().opt()]),
        reads=[("k_c",)], writes=[("g_k",)])
    P.col("ag5", lambda e: e.collective_compute(
        "AllGather", ALU.bypass, replica_groups=[[0, 1, 2, 3], [4, 5, 6, 7]],
        ins=[vp_c.ap().opt()], outs=[g_vp.ap().opt()]),
        reads=[("vp_c",)], writes=[("g_vp",)])

    ctxC1.close()
    ctxC2 = ExitStack()
    sbC2 = lambda n, s, d: ctxC2.enter_context(nc.sbuf_tensor(n, s, d))
    kb = [sbC2("kb0", [128, 4096], BF16), sbC2("kb1", [128, 4096], BF16)]
    vb = [sbC2("vb0", [128, 4 * 520], BF16), sbC2("vb1", [128, 4 * 520], BF16)]
    pr = [sbC2("pr0", [128, 1024], BF16), sbC2("pr1", [128, 1024], BF16)]
    kbk = ["kb0", "kb1"]
    vbk = ["vb0", "vb1"]
    prk = ["pr0", "pr1"]
    r_sb, k_rsb = lnC[0], "st_sumC"
    rbc, k_rbc = lnC[5], "bc_mrsC"

    def load_kv_head(kv, ki, vi):
        for rr in range(4):
            P.ins("sync", lambda e, rr=rr, kv=kv, ki=ki: e.dma_start(
                kb[ki][0:64, rr * 1024:(rr + 1) * 1024],
                g_k.ap()[rr * 512 + kv * 64: rr * 512 + (kv + 1) * 64, :]),
                reads=[("g_k",)], writes=[(kbk[ki],)])
            P.ins("sync", lambda e, rr=rr, kv=kv, ki=ki: e.dma_start(
                kb[ki][64:128, rr * 1024:(rr + 1) * 1024],
                g_k.ap()[rr * 512 + kv * 64: rr * 512 + (kv + 1) * 64, :]),
                reads=[("g_k",)], writes=[(kbk[ki],)])
            P.ins("sync", lambda e, rr=rr, kv=kv, vi=vi: e.dma_start(
                vb[vi][:, rr * 520:(rr + 1) * 520],
                g_vp.ap()[rr * 128:(rr + 1) * 128, kv * 520:(kv + 1) * 520]),
                reads=[("g_vp",)], writes=[(vbk[vi],)])

    load_kv_head(0, 0, 0)
    for kv in range(NKV):
        ki = vi = kv % 2
        if kv + 1 < NKV:
            load_kv_head(kv + 1, (kv + 1) % 2, (kv + 1) % 2)
        for g in range(4):
            h = kv * 4 + g
            pb_half = (h % 2) * 64
            ec = h // 2
            for lc in range(32):
                pS, sk = (pA, "pA") if lc % 2 == 0 else (pB, "pB")
                prb, pk = pr[lc % 2], prk[lc % 2]
                for th in range(2):
                    P.ins("tensor", lambda e, lc=lc, th=th, pS=pS, ki=ki,
                          pb_half=pb_half, ec=ec: e.matmul(
                        pS[:, th * 512:(th + 1) * 512],
                        kb[ki][pb_half:pb_half + 64, lc * 128:(lc + 1) * 128],
                        q_sb[pb_half:pb_half + 64,
                             ec * 1024 + th * 512: ec * 1024 + (th + 1) * 512],
                        start=True, stop=True, skip_group_check=True),
                        reads=[(kbk[ki],), ("q_sb", ec)], writes=[(sk, th)])
                P.ins("scalar", lambda e, pS=pS, prb=prb: e.activation(
                    prb[:], pS[:, :], AF.Exp, scale=np.float32(0.125)),
                    reads=[(sk, 0), (sk, 1)], writes=[(pk,)])
                for th in range(2):
                    P.ins("tensor", lambda e, lc=lc, th=th, vi=vi, prb=prb:
                          e.matmul(
                        pC[0:65, th * 512:(th + 1) * 512],
                        vb[vi][:, (lc // 8) * 520 + (lc % 8) * 65:
                               (lc // 8) * 520 + (lc % 8) * 65 + 65],
                        prb[:, th * 512:(th + 1) * 512],
                        start=(lc == 0), stop=(lc == 31), skip_group_check=True),
                        reads=[(vbk[vi],), (pk,), ("pC", th)],
                        writes=[("pC", th)])
            P.ins("vector", lambda e: e.reciprocal(r_sb[:], pC[64:65, :]),
                  reads=list(C01), writes=[(k_rsb,)])
            for th in range(2):
                P.ins("tensor", lambda e, th=th: e.matmul(
                    pB[0:64, 0:512], c_bc64[:],
                    r_sb[0:1, th * 512:(th + 1) * 512], start=True, stop=True,
                    skip_group_check=True),
                    reads=[("c_bc64",), (k_rsb,)], writes=[("pB", 0)])
                P.ins("scalar", lambda e, th=th: e.copy(
                    rbc[0:64, th * 512:(th + 1) * 512], pB[0:64, 0:512]),
                    reads=[("pB", 0)], writes=[(k_rbc,)])
            P.ins("vector", lambda e, ec=ec, pb_half=pb_half: e.tensor_tensor(
                xn_att[pb_half:pb_half + 64, ec * 1024:(ec + 1) * 1024],
                pC[0:64, :], rbc[0:64, :], ALU.mult),
                reads=list(C01) + [(k_rbc,)], writes=[("xn_att", ec)])

    cur, ck = wq_b, "wq_b"
    nxt, nk = wq_b2, "wq_b2"
    load_att_chunk(cur, ck, 24)
    for ec in range(16):
        if ec + 1 < 16:
            load_att_chunk(nxt, nk, 24 + ec + 1)
        pW, wk = (pA, "pA") if ec % 2 == 0 else (pB, "pB")
        for th in range(2):
            for hcc in range(HC):
                P.ins("tensor", lambda e, th=th, hcc=hcc, cur=cur, pW=pW: e.matmul(
                    pW[:, th * 512:(th + 1) * 512],
                    cur[:, hcc * 128:(hcc + 1) * 128],
                    xn_att[:, hcc * 1024 + th * 512: hcc * 1024 + (th + 1) * 512],
                    start=(hcc == 0), stop=(hcc == HC - 1), skip_group_check=True),
                    reads=[(ck,), ("xn_att", hcc), (wk, th)], writes=[(wk, th)])
        P.ins("vector", lambda e, ec=ec, pW=pW: e.tensor_tensor(
            x_sb[:, ec * 1024:(ec + 1) * 1024], pW[:, :],
            x_sb[:, ec * 1024:(ec + 1) * 1024], ALU.add),
            reads=[(wk, 0), (wk, 1), ("x_sb", ec)], writes=[("x_sb", ec)])
        cur, ck, nxt, nk = nxt, nk, cur, ck
    P.barrier()
    ctxC2.close()
    ctxC.close()

    ctxD = ExitStack()
    sbD = lambda n, s, d: ctxD.enter_context(nc.sbuf_tensor(n, s, d))
    lnD = (sbD("st_sumD", [1, 1024], F32), sbD("st_sqD", [1, 1024], F32),
           sbD("st_rstdD", [1, 1024], F32), sbD("st_mrsD", [1, 1024], F32),
           sbD("bc_rstdD", [128, 1024], F32), sbD("bc_mrsD", [128, 1024], F32),
           sbD("ln_tmpD", [128, 1024], F32))
    lnDk = ("st_sumD", "st_sqD", "st_rstdD", "st_mrsD",
            "bc_rstdD", "bc_mrsD", "ln_tmpD")
    xn2 = sbD("xn2", [128, HC * 1024], BF16)
    wg2 = sbD("wg2", [128, HC * 512], BF16)
    wu2 = sbD("wu2", [128, HC * 512], BF16)
    wd2 = sbD("wd2", [128, HC * 512], BF16)
    int2 = sbD("int2", [128, 4 * 1024], BF16)
    gl2 = (sbD("glg2", [128, 1024], F32), sbD("gla2", [128, 1024], F32),
           sbD("glb2", [128, 1024], F32), sbD("glc2", [128, 1024], F32))

    emit_ln(xn2, "xn2", lnD, lnDk)
    emit_ffn(xn2, "xn2", wg2, wu2, wd2, int2, gl2,
             ("wg2", "wu2", "wd2", "int2", "glg2", "gla2", "glb2", "glc2"))
    P.barrier()
    ctxD.close()

    ctxE = ExitStack()
    out_stage = ctxE.enter_context(nc.sbuf_tensor("out_stage", [128, 8 * D], F32))
    n_to = [0]
    for tc in range(8):
        for hc in range(HC):
            pT, pk = (pA, "pA") if n_to[0] % 2 == 0 else (pB, "pB")
            P.ins("tensor", lambda e, tc=tc, hc=hc, pT=pT: e.transpose(
                pT[:, 0:128],
                x_sb[:, hc * 1024 + tc * 128: hc * 1024 + tc * 128 + 128],
                c_id_f[:]),
                reads=[("x_sb", hc), ("c_id_f",)], writes=[(pk, 0)])
            P.ins("vector", lambda e, tc=tc, hc=hc, pT=pT: e.tensor_copy(
                out_stage[:, tc * D + hc * 128: tc * D + hc * 128 + 128],
                pT[:, 0:128]),
                reads=[(pk, 0)], writes=[("out_stage", tc)])
            n_to[0] += 1
    P.ins("gpsimd", lambda e: e.dma_start(
        out.ap().rearrange("(tc p) h -> p tc h", p=128),
        out_stage[:].rearrange("p (tc h) -> p tc h", tc=8)),
        reads=[("out_stage", tc) for tc in range(8)], writes=[("out",)])
    P.final_wait("gpsimd", "out")
    ctxE.close()

    block = ctx.enter_context(nc.Block())

    @block.sync
    def _(sync):
        P.run("sync", sync)

    @block.scalar
    def _(scalar):
        P.run("scalar", scalar)

    @block.vector
    def _(vector):
        P.run("vector", vector)

    @block.tensor
    def _(tensor):
        P.run("tensor", tensor)

    @block.gpsimd
    def _(gpsimd):
        P.run("gpsimd", gpsimd)

    ctx.close()
    return nc


def prepare_in_maps(L0_attn_x, L0_attn_Wq, L0_attn_Wk, L0_attn_Wv, L0_attn_Wo,
                    L0_ffn_gate_W_packed, L0_ffn_gate_scales,
                    L0_ffn_up_W_packed, L0_ffn_up_scales,
                    L0_ffn_down_W_packed, L0_ffn_down_scales):
    x = np.ascontiguousarray(np.asarray(L0_attn_x, np.float32).reshape(8192, D))
    Wq = np.asarray(L0_attn_Wq, np.float32)
    Wk = np.asarray(L0_attn_Wk, np.float32)
    Wv = np.asarray(L0_attn_Wv, np.float32)
    Wo = np.asarray(L0_attn_Wo, np.float32)
    gpk = np.ascontiguousarray(np.asarray(L0_ffn_gate_W_packed, np.int32))
    gscn = np.ascontiguousarray(np.asarray(L0_ffn_gate_scales, np.float32))
    upkk = np.ascontiguousarray(np.asarray(L0_ffn_up_W_packed, np.int32))
    uscn = np.ascontiguousarray(np.asarray(L0_ffn_up_scales, np.float32))
    dpk = np.asarray(L0_ffn_down_W_packed, np.int32)
    dscn = np.asarray(L0_ffn_down_scales, np.float32)

    aw_full = np.ascontiguousarray(np.concatenate([Wq, Wk, Wv, Wo], axis=0))

    half = 32
    inv_freq = 1.0 / (10000.0 ** (np.arange(half, dtype=np.float64) / half))

    in_maps = []
    for c in range(NCORES):
        pos = (c % 4) * TOK + np.arange(TOK, dtype=np.float64)
        ang = pos[None, :] * inv_freq[:, None]
        cs = np.cos(ang).astype(np.float32)
        sn = np.sin(ang).astype(np.float32)
        in_maps.append(dict(
            x_in=x[c * TOK:(c + 1) * TOK],
            gp=gpk[c * 2048:(c + 1) * 2048],
            gsc=gscn[c * 2048:(c + 1) * 2048],
            up_p=upkk[c * 2048:(c + 1) * 2048],
            usc=uscn[c * 2048:(c + 1) * 2048],
            dp=np.ascontiguousarray(dpk[:, c * 1024:(c + 1) * 1024]),
            dsc=np.ascontiguousarray(dscn[:, c * 64:(c + 1) * 64]),
            aw=np.ascontiguousarray(aw_full[c * 640:(c + 1) * 640]),
            cosq=np.ascontiguousarray(np.concatenate([cs, cs, cs, cs], axis=0)),
            sinq=np.ascontiguousarray(np.concatenate([sn, sn, sn, sn], axis=0)),
        ))
    return in_maps


def kernel(**inputs):
    in_maps = prepare_in_maps(**inputs)
    nc = build_program()
    res = run_bass_kernel_spmd(nc, in_maps, core_ids=list(range(NCORES)))
    outs = [np.asarray(res.results[c]["out"], np.float32) for c in range(NCORES)]
    return np.concatenate(outs, axis=0).reshape(2, SEQ, D)


# version 04
# speedup vs baseline: 1.1557x; 1.1557x over previous
import numpy as np
import ml_dtypes
import concourse.bass as bass
import concourse.mybir as mybir
from concourse.bass_utils import run_bass_kernel_spmd
from contextlib import ExitStack

F32 = mybir.dt.float32
BF16 = mybir.dt.bfloat16
I32 = mybir.dt.int32
I16 = mybir.dt.int16
AF = mybir.ActivationFunctionType
ALU = mybir.AluOpType

NCORES = 8
TOK = 1024
D = 2048
HC = 16
SEQ = 4096
NKV = 8
GELU_B = 0.044715
GELU_A = 0.7978845608028654

CROWS_G = 4 * 2048
CROWS_A = 5 * 512
OFF_WG, OFF_WU, OFF_WD, OFF_AT = 0, CROWS_G, 2 * CROWS_G, 3 * CROWS_G
CROWS = 3 * CROWS_G + CROWS_A


class Prog:
    ENGS = ("sync", "scalar", "vector", "tensor", "gpsimd")
    CENGS = ("scalar", "vector", "tensor")
    INC = {"scalar": 1, "vector": 1, "tensor": 1}

    def __init__(self, mksem):
        self.mksem = mksem
        self.ops = {e: [] for e in self.ENGS}
        self.cnt = {e: 0 for e in self.CENGS}
        self.W = {e: {} for e in self.ENGS}
        self.snap = {}
        self.state = {}
        self.ctr = {e: mksem("ctr_" + e) for e in self.CENGS}
        self.psem = {}
        self.stream = {}
        self.n_waits = 0

    def _deps(self, reads, writes):
        deps = {}
        for k in reads:
            st = self.state.get(k)
            if st is not None and st[0] is not None:
                src, idx = st[0]
                if deps.get(src, 0) < idx:
                    deps[src] = idx
        for k in writes:
            st = self.state.get(k)
            if st is not None:
                if st[0] is not None:
                    src, idx = st[0]
                    if deps.get(src, 0) < idx:
                        deps[src] = idx
                for src, idx in st[1].items():
                    if deps.get(src, 0) < idx:
                        deps[src] = idx
        return deps

    def _emit_waits(self, eng, deps):
        Wm = self.W[eng]
        for src in sorted(deps):
            idx = deps[src]
            if src == "tensor" and eng == "tensor":
                continue
            if src in self.stream:
                idx = self.stream[src][1]
            if Wm.get(src, 0) >= idx:
                continue
            if src in self.psem:
                self.ops[eng].append(
                    lambda e, h=self.psem[src]: e.wait_ge(h, 1))
            elif src in self.stream:
                v = idx * 16
                self.ops[eng].append(
                    lambda e, h=self.stream[src][0], v=v: e.wait_ge(h, v))
            else:
                v = idx * self.INC[src]
                self.ops[eng].append(
                    lambda e, h=self.ctr[src], v=v: e.wait_ge(h, v))
            self.n_waits += 1
            sn = self.snap.get((src, idx))
            if sn:
                for s2, i2 in sn.items():
                    if Wm.get(s2, 0) < i2:
                        Wm[s2] = i2
            if Wm.get(src, 0) < idx:
                Wm[src] = idx

    def _post(self, name, eng, myidx, reads, writes):
        for k in reads:
            st = self.state.get(k)
            if st is None:
                st = self.state[k] = [None, {}]
            st[1][name] = myidx
        for k in writes:
            self.state[k] = [(name, myidx), {}]
        self.snap[(name, myidx)] = dict(self.W[eng])

    def ins(self, eng, fn, reads=(), writes=(), stream=None):
        deps = self._deps(reads, writes)
        self._emit_waits(eng, deps)
        if eng in self.CENGS:
            self.cnt[eng] += 1
            myidx = self.cnt[eng]
            h = self.ctr[eng]
            self.ops[eng].append(lambda e, f=fn, h=h: f(e).then_inc(h, 1))
            self._post(eng, eng, myidx, reads, writes)
        else:
            sname = stream if stream is not None else writes[0][0]
            st = self.stream.get(sname)
            if st is None:
                st = self.stream[sname] = [self.mksem("dma_" + sname), 0]
            st[1] += 1
            myidx = st[1]
            self.ops[eng].append(
                lambda e, f=fn, h=st[0]: f(e).then_inc(h, 16))
            self._post(sname, eng, myidx, reads, writes)

    def col(self, psname, fn, reads=(), writes=()):
        deps = self._deps(reads, writes)
        self._emit_waits("gpsimd", deps)
        h = self.psem[psname]
        self.ops["gpsimd"].append(lambda e, f=fn, h=h: f(e).then_inc(h, 1))
        self._post(psname, "gpsimd", 1, reads, writes)

    def barrier(self):
        targets = [(f, self.cnt[f], self.ctr[f], self.INC[f])
                   for f in self.CENGS if self.cnt[f] > 0]
        targets += [(s, st[1], st[0], 16)
                    for s, st in self.stream.items() if st[1] > 0]
        for e in self.ENGS:
            Wm = self.W[e]
            for f, n, h, inc in targets:
                if f == e or Wm.get(f, 0) >= n:
                    continue
                v = n * inc
                self.ops[e].append(lambda en, h=h, v=v: en.wait_ge(h, v))
                Wm[f] = n
                self.n_waits += 1

    def final_wait(self, eng, sname):
        st = self.stream[sname]
        v = st[1] * 16
        self.ops[eng].append(lambda e, h=st[0], v=v: e.wait_ge(h, v))

    def run(self, eng_name, e):
        for f in self.ops[eng_name]:
            f(e)


def build_program():
    nc = bass.Bass("TRN2", target_bir_lowering=False)

    x_in = nc.dram_tensor("x_in", [TOK, D], F32, kind="ExternalInput")
    gp = nc.dram_tensor("gp", [2048, 1024], I32, kind="ExternalInput")
    gsc = nc.dram_tensor("gsc", [2048, 64], F32, kind="ExternalInput")
    up_p = nc.dram_tensor("up_p", [2048, 1024], I32, kind="ExternalInput")
    usc = nc.dram_tensor("usc", [2048, 64], F32, kind="ExternalInput")
    dp = nc.dram_tensor("dp", [2048, 1024], I32, kind="ExternalInput")
    dsc = nc.dram_tensor("dsc", [2048, 64], F32, kind="ExternalInput")
    aw = nc.dram_tensor("aw", [640, D], F32, kind="ExternalInput")
    cosq = nc.dram_tensor("cosq", [128, TOK], F32, kind="ExternalInput")
    sinq = nc.dram_tensor("sinq", [128, TOK], F32, kind="ExternalInput")
    out = nc.dram_tensor("out", [TOK, D], F32, kind="ExternalOutput")

    contrib = nc.dram_tensor("contrib", [CROWS, 512], BF16)
    g_w = nc.dram_tensor("g_w", [NCORES * CROWS, 512], BF16,
                         addr_space="Shared")
    k_c = nc.dram_tensor("k_c", [512, TOK], BF16)
    vp_c = nc.dram_tensor("vp_c", [128, NKV * 8 * 65], BF16)
    g_k = nc.dram_tensor("g_k", [4 * 512, TOK], BF16)
    g_vp = nc.dram_tensor("g_vp", [4 * 128, NKV * 8 * 65], BF16)

    def gw_off(mat, r, b):
        return (mat * NCORES * CROWS_G + r * CROWS_G + b * 2048) * 512

    def gwd_base(r):
        return (2 * NCORES * CROWS_G + r * CROWS_G) * 512

    def gatt_off(i):
        r, j = i // 5, i % 5
        return (3 * NCORES * CROWS_G + r * CROWS_A + j * 512) * 512

    identF = nc.inline_tensor(np.eye(128, dtype=np.float32), "identF")
    identB = nc.inline_tensor(np.eye(128).astype(ml_dtypes.bfloat16), "identB")
    ropeT_np = np.zeros((128, 128), np.float32)
    for b in range(2):
        for i in range(32):
            ropeT_np[b * 64 + i + 32, b * 64 + i] = -1.0
            ropeT_np[b * 64 + i, b * 64 + i + 32] = 1.0
    ropeT = nc.inline_tensor(ropeT_np, "ropeT")
    ones128 = nc.inline_tensor(np.ones((128, 1), np.float32), "ones128")
    bc128 = nc.inline_tensor(np.ones((1, 128), np.float32), "bc128")
    bc64 = nc.inline_tensor(np.ones((1, 64), np.float32), "bc64")

    ctx = ExitStack()
    sb = lambda n, s, d: ctx.enter_context(nc.sbuf_tensor(n, s, d))
    ps = lambda n, s, d: ctx.enter_context(nc.psum_tensor(n, s, d))
    mksem = lambda n: ctx.enter_context(nc.semaphore(n))

    P = Prog(mksem)
    for a in range(6):
        P.psem["ag%d" % a] = mksem("ag%d" % a)

    pA = ps("pA", [128, 1024], F32)
    pB = ps("pB", [128, 1024], F32)
    pC = ps("pC", [128, 1024], F32)
    pEa = ps("pEa", [128, 512], BF16)
    pEb = ps("pEb", [128, 512], BF16)
    pE = [pEa, pEb]
    pEk = ["pEa", "pEb"]
    A01 = (("pA", 0), ("pA", 1))
    B01 = (("pB", 0), ("pB", 1))
    C01 = (("pC", 0), ("pC", 1))

    x_sb = sb("x_sb", [128, HC * 1024], F32)
    c_id_f = sb("c_id_f", [128, 128], F32)
    c_id_b = sb("c_id_b", [128, 128], BF16)
    c_rope = sb("c_rope", [128, 128], F32)
    c_ones = sb("c_ones", [128, 1], F32)
    c_bc128 = sb("c_bc128", [1, 128], F32)
    c_bc64 = sb("c_bc64", [1, 64], F32)
    c_cos = sb("c_cos", [128, TOK], F32)
    c_sin = sb("c_sin", [128, TOK], F32)
    xn = sb("xn", [128, HC * 1024], BF16)
    st_sum = sb("st_sum", [1, 1024], F32)
    st_sq = sb("st_sq", [1, 1024], F32)
    st_rstd = sb("st_rstd", [1, 1024], F32)
    st_mrs = sb("st_mrs", [1, 1024], F32)
    bc_rstd = sb("bc_rstd", [128, 1024], F32)
    bc_mrs = sb("bc_mrs", [128, 1024], F32)
    ln_tmp = sb("ln_tmp", [128, 1024], F32)

    for dst, src, key in [
            (c_id_f, identF, "c_id_f"), (c_id_b, identB, "c_id_b"),
            (c_rope, ropeT, "c_rope"), (c_ones, ones128, "c_ones"),
            (c_bc128, bc128, "c_bc128"), (c_bc64, bc64, "c_bc64"),
            (c_cos, cosq, "c_cos"), (c_sin, sinq, "c_sin")]:
        P.ins("sync", lambda e, d=dst, s=src: e.dma_start(d[:], s.ap()),
              writes=[(key,)], stream="consts")

    def emit_ln():
        for th in range(2):
            for hc in range(HC):
                P.ins("tensor", lambda e, hc=hc, th=th: e.matmul(
                    pA[0:1, th * 512:(th + 1) * 512], c_ones[:],
                    x_sb[:, hc * 1024 + th * 512: hc * 1024 + (th + 1) * 512],
                    start=(hc == 0), stop=(hc == HC - 1), skip_group_check=True),
                    reads=[("c_ones",), ("x_sb", hc), ("pA", th)],
                    writes=[("pA", th)])
        for hc in range(HC):
            P.ins("scalar", lambda e, hc=hc: e.activation(
                ln_tmp[:], x_sb[:, hc * 1024:(hc + 1) * 1024], AF.Square),
                reads=[("x_sb", hc)], writes=[("ln_tmp",)])
            for th in range(2):
                P.ins("tensor", lambda e, hc=hc, th=th: e.matmul(
                    pB[0:1, th * 512:(th + 1) * 512], c_ones[:],
                    ln_tmp[:, th * 512:(th + 1) * 512],
                    start=(hc == 0), stop=(hc == HC - 1), skip_group_check=True),
                    reads=[("c_ones",), ("ln_tmp",), ("pB", th)],
                    writes=[("pB", th)])
        P.ins("vector", lambda e: e.tensor_single_scalar(
            st_sum[:], pA[0:1, :], 1.0 / D, ALU.mult),
            reads=list(A01), writes=[("st_sum",)])
        P.ins("vector", lambda e: e.tensor_single_scalar(
            st_sq[:], pB[0:1, :], 1.0 / D, ALU.mult),
            reads=list(B01), writes=[("st_sq",)])
        P.ins("vector", lambda e: e.tensor_mul(st_rstd[:], st_sum[:], st_sum[:]),
              reads=[("st_sum",)], writes=[("st_rstd",)])
        P.ins("vector", lambda e: e.tensor_sub(st_rstd[:], st_sq[:], st_rstd[:]),
              reads=[("st_sq",), ("st_rstd",)], writes=[("st_rstd",)])
        P.ins("scalar", lambda e: e.activation(st_rstd[:], st_rstd[:], AF.Sqrt,
                                               bias=np.float32(1e-5)),
              reads=[("st_rstd",)], writes=[("st_rstd",)])
        P.ins("vector", lambda e: e.reciprocal(st_rstd[:], st_rstd[:]),
              reads=[("st_rstd",)], writes=[("st_rstd",)])
        P.ins("vector", lambda e: e.tensor_mul(st_mrs[:], st_sum[:], st_rstd[:]),
              reads=[("st_sum",), ("st_rstd",)], writes=[("st_mrs",)])
        for th in range(2):
            P.ins("tensor", lambda e, th=th: e.matmul(
                pA[:, th * 512:(th + 1) * 512], c_bc128[:],
                st_rstd[0:1, th * 512:(th + 1) * 512], start=True, stop=True,
                skip_group_check=True),
                reads=[("c_bc128",), ("st_rstd",)], writes=[("pA", th)])
            P.ins("tensor", lambda e, th=th: e.matmul(
                pB[:, th * 512:(th + 1) * 512], c_bc128[:],
                st_mrs[0:1, th * 512:(th + 1) * 512], start=True, stop=True,
                skip_group_check=True),
                reads=[("c_bc128",), ("st_mrs",)], writes=[("pB", th)])
        P.ins("scalar", lambda e: e.copy(bc_rstd[:], pA[:, :]),
              reads=list(A01), writes=[("bc_rstd",)])
        P.ins("scalar", lambda e: e.copy(bc_mrs[:], pB[:, :]),
              reads=list(B01), writes=[("bc_mrs",)])
        for hc in range(HC):
            P.ins("vector", lambda e, hc=hc: e.tensor_tensor(
                ln_tmp[:], x_sb[:, hc * 1024:(hc + 1) * 1024], bc_rstd[:],
                ALU.mult),
                reads=[("x_sb", hc), ("bc_rstd",)], writes=[("ln_tmp",)])
            P.ins("vector", lambda e, hc=hc: e.tensor_tensor(
                xn[:, hc * 1024:(hc + 1) * 1024], ln_tmp[:], bc_mrs[:],
                ALU.subtract),
                reads=[("ln_tmp",), ("bc_mrs",)], writes=[("xn", hc)])

    def emit_ffn(wg_b, wu_b, wd_b, inter, gl, keys):
        gl_g, gl_a, gl_b, gl_c = gl
        kwg, kwu, kwd, kint, kg, ka, kb_, kc = keys
        for s in range(32):
            r, b = s // 4, s % 4
            P.ins("sync", lambda e, r=r, b=b: e.dma_start(
                wg_b[:], bass.AP(g_w, gw_off(0, r, b), [[8192, 128], [1, 8192]])),
                reads=[("g_w", "g")], writes=[(kwg,)])
            P.ins("sync", lambda e, r=r, b=b: e.dma_start(
                wu_b[:], bass.AP(g_w, gw_off(1, r, b), [[8192, 128], [1, 8192]])),
                reads=[("g_w", "u")], writes=[(kwu,)])
            P.ins("sync", lambda e, r=r, b=b: e.dma_start(
                wd_b[:].rearrange("p (cc n) -> p cc n", cc=16),
                bass.AP(g_w, gwd_base(r) + 512 * b,
                        [[2048, 128], [262144, 16], [1, 512]])),
                reads=[("g_w", "d")], writes=[(kwd,)])
            for nn in range(4):
                for th in range(2):
                    for hcc in range(HC):
                        P.ins("tensor", lambda e, nn=nn, th=th, hcc=hcc: e.matmul(
                            pA[:, th * 512:(th + 1) * 512],
                            wg_b[:, hcc * 512 + nn * 128: hcc * 512 + nn * 128 + 128],
                            xn[:, hcc * 1024 + th * 512: hcc * 1024 + (th + 1) * 512],
                            start=(hcc == 0), stop=(hcc == HC - 1),
                            skip_group_check=True),
                            reads=[(kwg,), ("xn", hcc), ("pA", th)],
                            writes=[("pA", th)])
                for th in range(2):
                    for hcc in range(HC):
                        P.ins("tensor", lambda e, nn=nn, th=th, hcc=hcc: e.matmul(
                            pB[:, th * 512:(th + 1) * 512],
                            wu_b[:, hcc * 512 + nn * 128: hcc * 512 + nn * 128 + 128],
                            xn[:, hcc * 1024 + th * 512: hcc * 1024 + (th + 1) * 512],
                            start=(hcc == 0), stop=(hcc == HC - 1),
                            skip_group_check=True),
                            reads=[(kwu,), ("xn", hcc), ("pB", th)],
                            writes=[("pB", th)])
                P.ins("scalar", lambda e: e.copy(gl_g[:], pA[:, :]),
                      reads=list(A01), writes=[(kg,)])
                P.ins("scalar", lambda e: e.activation(gl_a[:], gl_g[:], AF.Square),
                      reads=[(kg,)], writes=[(ka,)])
                P.ins("vector", lambda e: e.tensor_scalar(
                    gl_b[:], gl_a[:], GELU_B, 1.0, ALU.mult, ALU.add),
                    reads=[(ka,)], writes=[(kb_,)])
                P.ins("vector", lambda e: e.tensor_mul(gl_a[:], gl_g[:], gl_b[:]),
                      reads=[(kg,), (kb_,)], writes=[(ka,)])
                P.ins("scalar", lambda e: e.activation(
                    gl_b[:], gl_a[:], AF.Tanh, scale=np.float32(GELU_A)),
                    reads=[(ka,)], writes=[(kb_,)])
                P.ins("scalar", lambda e: e.copy(gl_c[:], pB[:, :]),
                      reads=list(B01), writes=[(kc,)])
                P.ins("vector", lambda e: e.tensor_mul(gl_a[:], gl_g[:], gl_b[:]),
                      reads=[(kg,), (kb_,)], writes=[(ka,)])
                P.ins("vector", lambda e: e.tensor_add(gl_b[:], gl_a[:], gl_g[:]),
                      reads=[(ka,), (kg,)], writes=[(kb_,)])
                P.ins("vector", lambda e, nn=nn: e.tensor_mul(
                    inter[:, nn * 1024:(nn + 1) * 1024], gl_b[:], gl_c[:]),
                    reads=[(kb_,), (kc,)], writes=[(kint, nn)])
            for hcc in range(HC):
                pD, D01 = (pC, C01) if hcc % 2 == 0 else (pB, B01)
                for th in range(2):
                    for nn in range(4):
                        P.ins("tensor", lambda e, hcc=hcc, th=th, nn=nn, pD=pD:
                              e.matmul(
                            pD[:, th * 512:(th + 1) * 512],
                            wd_b[:, hcc * 512 + nn * 128: hcc * 512 + nn * 128 + 128],
                            inter[:, nn * 1024 + th * 512: nn * 1024 + (th + 1) * 512],
                            start=(nn == 0), stop=(nn == 3), skip_group_check=True),
                            reads=[(kwd,), (kint, nn), D01[th]],
                            writes=[D01[th]])
                P.ins("vector", lambda e, hcc=hcc, pD=pD: e.scalar_tensor_tensor(
                    x_sb[:, hcc * 1024:(hcc + 1) * 1024], pD[:, :], 0.25,
                    x_sb[:, hcc * 1024:(hcc + 1) * 1024], ALU.mult, ALU.add),
                    reads=list(D01) + [("x_sb", hcc)], writes=[("x_sb", hcc)])

    ctxX = ExitStack()
    xs = [ctxX.enter_context(nc.sbuf_tensor("x_stage_a", [128, D], F32)),
          ctxX.enter_context(nc.sbuf_tensor("x_stage_b", [128, D], F32))]
    xsk = ["x_stage_a", "x_stage_b"]

    n_tr = [0]
    for tc in range(8):
        buf, bk = xs[tc % 2], xsk[tc % 2]
        P.ins("sync", lambda e, tc=tc, buf=buf: e.dma_start(
            buf[:], x_in.ap()[tc * 128:(tc + 1) * 128, :]),
            writes=[(bk,)])
        for hc in range(HC):
            pT, pk = (pA, "pA") if n_tr[0] % 2 == 0 else (pB, "pB")
            P.ins("tensor", lambda e, hc=hc, buf=buf, pT=pT: e.transpose(
                pT[:, 0:128], buf[:, hc * 128: hc * 128 + 128], c_id_f[:]),
                reads=[(bk,), ("c_id_f",)], writes=[(pk, 0)])
            P.ins("vector", lambda e, hc=hc, tc=tc, pT=pT: e.tensor_copy(
                x_sb[:, hc * 1024 + tc * 128: hc * 1024 + tc * 128 + 128],
                pT[:, 0:128]),
                reads=[(pk, 0)], writes=[("x_sb", hc)])
            n_tr[0] += 1

    emit_ln()

    ctxX.close()
    ctxA = ExitStack()
    sbA = lambda n, s, d: ctxA.enter_context(nc.sbuf_tensor(n, s, d))
    dqset = []
    for sfx in ("a", "b"):
        dqset.append(dict(
            pk32=sbA("dq_pk32" + sfx, [128, 1024], I32),
            p16=sbA("dq_p16" + sfx, [128, 1024], I16),
            m=sbA("dq_m" + sfx, [128, 1024], I16),
            sg=sbA("dq_sg" + sfx, [128, 1024], I16),
            u=sbA("dq_u" + sfx, [128, 1024], BF16),
            r=sbA("dq_r" + sfx, [128, 1024], BF16),
            w=sbA("dq_w" + sfx, [128, 1024], BF16),
            w2=sbA("dq_w2" + sfx, [128, 1024], BF16),
            sc=sbA("dq_sc" + sfx, [128, 64], F32),
            wnat=sbA("wnat" + sfx, [128, 2048], BF16),
            sfx=sfx,
        ))
    c_sb = sbA("c_sb", [128, HC * 512], BF16)
    aw_sb = sbA("aw_sb", [128, D], F32)
    aw_b = sbA("aw_b", [128, D], BF16)

    n_dq = [0]
    n_trw = [0]

    def emit_dq_chunk(pk_t, sc_t, cc, first=False):
        q = dqset[n_dq[0] % 2]
        n_dq[0] += 1
        s = q["sfx"]
        extra = [(k,) for k in xsk] if first else []
        P.ins("gpsimd", lambda e, cc=cc, pk_t=pk_t, q=q: e.dma_start(
            q["pk32"][:], pk_t.ap()[cc * 128:(cc + 1) * 128, :]),
            writes=[("dq_pk32" + s,)] + extra, stream="dq_pk32" + s)
        P.ins("gpsimd", lambda e, cc=cc, sc_t=sc_t, q=q: e.dma_start(
            q["sc"][:], sc_t.ap()[cc * 128:(cc + 1) * 128, :]),
            writes=[("dq_sc" + s,)])
        P.ins("vector", lambda e, q=q: e.tensor_copy(q["p16"][:], q["pk32"][:]),
              reads=[("dq_pk32" + s,)], writes=[("dq_p16" + s,)])
        for half in range(2):
            if half == 0:
                P.ins("vector", lambda e, q=q: e.tensor_scalar(
                    q["m"][:], q["p16"][:], 7, 6,
                    ALU.bitwise_and, ALU.logical_shift_left),
                    reads=[("dq_p16" + s,)], writes=[("dq_m" + s,)])
                P.ins("vector", lambda e, q=q: e.tensor_scalar(
                    q["sg"][:], q["p16"][:], 8, 12,
                    ALU.bitwise_and, ALU.logical_shift_left),
                    reads=[("dq_p16" + s,)], writes=[("dq_sg" + s,)])
            else:
                P.ins("vector", lambda e, q=q: e.tensor_scalar(
                    q["m"][:], q["p16"][:], 2, 0x1C0,
                    ALU.logical_shift_left, ALU.bitwise_and),
                    reads=[("dq_p16" + s,)], writes=[("dq_m" + s,)])
                P.ins("vector", lambda e, q=q: e.tensor_scalar(
                    q["sg"][:], q["p16"][:], 128, 8,
                    ALU.bitwise_and, ALU.logical_shift_left),
                    reads=[("dq_p16" + s,)], writes=[("dq_sg" + s,)])
            P.ins("vector", lambda e, q=q: e.tensor_single_scalar(
                q["u"][:].bitcast(I16), q["m"][:], 0x3F00, ALU.add),
                reads=[("dq_m" + s,)], writes=[("dq_u" + s,)])
            P.ins("scalar", lambda e, q=q: e.activation(
                q["r"][:], q["u"][:], AF.Relu, bias=np.float32(1.0),
                scale=np.float32(-1.0)),
                reads=[("dq_u" + s,)], writes=[("dq_r" + s,)])
            wk = ("dq_w" if half == 0 else "dq_w2") + s
            wt = q["w"] if half == 0 else q["w2"]
            P.ins("vector", lambda e, wt=wt, q=q: e.tensor_sub(
                wt[:], q["u"][:], q["r"][:]),
                reads=[("dq_u" + s,), ("dq_r" + s,)], writes=[(wk,)])
            P.ins("vector", lambda e, wt=wt, q=q: e.tensor_add(
                wt[:].bitcast(I16), wt[:].bitcast(I16), q["sg"][:]),
                reads=[(wk,), ("dq_sg" + s,)], writes=[(wk,)])
        sc_bcast = bass.AP(q["sc"], 0, [[64, 128], [1, 64], [0, 16]])
        P.ins("vector", lambda e, q=q, sc_bcast=sc_bcast: e.tensor_tensor(
            q["wnat"][:, 0:2048:2], q["w"][:], sc_bcast, ALU.mult),
            reads=[("dq_w" + s,), ("dq_sc" + s,)], writes=[("wnat" + s,)])
        P.ins("vector", lambda e, q=q, sc_bcast=sc_bcast: e.tensor_tensor(
            q["wnat"][:, 1:2048:2], q["w2"][:], sc_bcast, ALU.mult),
            reads=[("dq_w2" + s,), ("dq_sc" + s,)], writes=[("wnat" + s,)])
        return q

    def emit_transposes(src_sb, src_key, dst_fn):
        for hh in range(16):
            pT, pk = pE[n_trw[0] % 2], pEk[n_trw[0] % 2]
            P.ins("tensor", lambda e, hh=hh, pT=pT, src=src_sb: e.transpose(
                pT[:, 0:128], src[:, hh * 128:(hh + 1) * 128], c_id_b[:]),
                reads=[(src_key,), ("c_id_b",)], writes=[(pk,)])
            P.ins("scalar", lambda e, hh=hh, pT=pT, f=dst_fn: e.copy(
                f(hh), pT[:, 0:128]),
                reads=[(pk,)], writes=[("c_sb",)])
            n_trw[0] += 1

    first_dq = [True]
    for mat, (pk_t, sc_t, off, gk) in enumerate(
            [(gp, gsc, OFF_WG, "g"), (up_p, usc, OFF_WU, "u")]):
        for cc in range(16):
            q = emit_dq_chunk(pk_t, sc_t, cc, first=first_dq[0])
            first_dq[0] = False
            emit_transposes(q["wnat"], "wnat" + q["sfx"],
                            lambda hh, cc=cc: c_sb[:, hh * 512 + (cc % 4) * 128:
                                                   hh * 512 + (cc % 4) * 128 + 128])
            if cc % 4 == 3:
                b = cc // 4
                P.ins("gpsimd", lambda e, off=off, b=b: e.dma_start(
                    bass.AP(contrib, (off + b * 2048) * 512,
                            [[8192, 128], [1, 8192]]),
                    c_sb[:]),
                    reads=[("c_sb",)], writes=[("contrib", gk, b)],
                    stream="contrib")
        P.col("ag%d" % mat, lambda e, off=off, mat=mat: e.collective_compute(
            "AllGather", ALU.bypass, replica_groups=[list(range(8))],
            ins=[contrib.ap()[off:off + CROWS_G, :].opt()],
            outs=[bass.AP(g_w, mat * NCORES * CROWS_G * 512,
                          [[512, NCORES * CROWS_G], [1, 512]])],
        ), reads=[("contrib", gk, b) for b in range(4)], writes=[("g_w", gk)])

    for cc in range(16):
        q = emit_dq_chunk(dp, dsc, cc)
        emit_transposes(q["wnat"], "wnat" + q["sfx"],
                        lambda hh: c_sb[:, hh * 128:(hh + 1) * 128])
        P.ins("gpsimd", lambda e, cc=cc: e.dma_start(
            bass.AP(contrib, (OFF_WD + cc * 512) * 512, [[2048, 128], [1, 2048]]),
            c_sb[:, 0:2048]),
            reads=[("c_sb",)], writes=[("contrib", "d", cc)], stream="contrib")
    P.col("ag2", lambda e: e.collective_compute(
        "AllGather", ALU.bypass, replica_groups=[list(range(8))],
        ins=[contrib.ap()[OFF_WD:OFF_WD + CROWS_G, :].opt()],
        outs=[bass.AP(g_w, 2 * NCORES * CROWS_G * 512,
                      [[512, NCORES * CROWS_G], [1, 512]])],
    ), reads=[("contrib", "d", cc) for cc in range(16)], writes=[("g_w", "d")])

    P.ins("sync", lambda e: e.dma_start(aw_sb[:], aw.ap()[0:128, :]),
          writes=[("aw_sb",), ("x_stage_a",), ("x_stage_b",)], stream="aw_sb")
    for j in range(5):
        P.ins("vector", lambda e: e.tensor_copy(aw_b[:], aw_sb[:]),
              reads=[("aw_sb",)], writes=[("aw_b",)])
        if j < 4:
            P.ins("sync", lambda e, j=j: e.dma_start(
                aw_sb[:], aw.ap()[(j + 1) * 128:(j + 2) * 128, :]),
                writes=[("aw_sb",)], stream="aw_sb")
        emit_transposes(aw_b, "aw_b",
                        lambda hh: c_sb[:, hh * 128:(hh + 1) * 128])
        P.ins("gpsimd", lambda e, j=j: e.dma_start(
            bass.AP(contrib, (OFF_AT + j * 512) * 512, [[2048, 128], [1, 2048]]),
            c_sb[:, 0:2048]),
            reads=[("c_sb",)], writes=[("contrib", "a", j)], stream="contrib")
    P.col("ag3", lambda e: e.collective_compute(
        "AllGather", ALU.bypass, replica_groups=[list(range(8))],
        ins=[contrib.ap()[OFF_AT:OFF_AT + CROWS_A, :].opt()],
        outs=[bass.AP(g_w, 3 * NCORES * CROWS_G * 512,
                      [[512, NCORES * CROWS_A], [1, 512]])],
    ), reads=[("contrib", "a", j) for j in range(5)], writes=[("g_w", "a")])

    P.barrier()
    ctxA.close()

    ctxB = ExitStack()
    sbB = lambda n, s, d: ctxB.enter_context(nc.sbuf_tensor(n, s, d))
    wg1 = sbB("wg1", [128, HC * 512], BF16)
    wu1 = sbB("wu1", [128, HC * 512], BF16)
    wd1 = sbB("wd1", [128, HC * 512], BF16)
    int1 = sbB("int1", [128, 4 * 1024], BF16)
    gl1 = (sbB("glg1", [128, 1024], F32), sbB("gla1", [128, 1024], F32),
           sbB("glb1", [128, 1024], F32), sbB("glc1", [128, 1024], F32))

    emit_ffn(wg1, wu1, wd1, int1, gl1,
             ("wg1", "wu1", "wd1", "int1", "glg1", "gla1", "glb1", "glc1"))
    P.barrier()
    ctxB.close()

    ctxC = ExitStack()
    sbC = lambda n, s, d: ctxC.enter_context(nc.sbuf_tensor(n, s, d))
    q_sb = sbC("q_sb", [128, HC * 1024], BF16)
    wq_b = sbC("wq_b", [128, HC * 128], BF16)
    wq_b2 = sbC("wq_b2", [128, HC * 128], BF16)
    rope_f, k_ropef = ln_tmp, "ln_tmp"
    rope_g, k_ropeg = bc_rstd, "bc_rstd"

    emit_ln()

    ctxC1 = ExitStack()
    sbC1 = lambda n, s, d: ctxC1.enter_context(nc.sbuf_tensor(n, s, d))
    kst = sbC1("kst", [128, 4 * 1024], BF16)
    vst = sbC1("vst", [128, NKV * 8 * 65], BF16)
    wv_sb = sbC1("wv_sb", [128, HC * 512], BF16)

    ones_ap = bass.AP(vst, 64, [[NKV * 8 * 65, 128], [520, NKV], [65, 8], [1, 1]])
    P.ins("vector", lambda e: e.memset(ones_ap, 1.0),
          writes=[("vst", tc) for tc in range(8)])

    for ec4 in range(4):
        P.ins("sync", lambda e, ec4=ec4: e.dma_start(
            bass.AP(wv_sb, ec4 * 128, [[HC * 512, 128], [512, 16], [1, 128]]),
            bass.AP(g_w, gatt_off(20 + ec4), [[2048, 128], [128, 16], [1, 128]])),
            reads=[("g_w", "a")], writes=[("wv_sb",)])

    def load_att_chunk(buf, bk, i):
        P.ins("sync", lambda e, buf=buf, i=i: e.dma_start(
            buf[:], bass.AP(g_w, gatt_off(i), [[2048, 128], [1, 2048]])),
            reads=[("g_w", "a")], writes=[(bk,)])

    qk_list = [(i, q_sb, "q_sb", i) for i in range(16)] + \
              [(16 + i, kst, "kst", i) for i in range(4)]
    cur, ck = wq_b, "wq_b"
    nxt, nk = wq_b2, "wq_b2"
    load_att_chunk(cur, ck, qk_list[0][0])
    for idx, (ci, dsb, dk, dj) in enumerate(qk_list):
        if idx + 1 < len(qk_list):
            load_att_chunk(nxt, nk, qk_list[idx + 1][0])
        for th in range(2):
            for hcc in range(HC):
                P.ins("tensor", lambda e, th=th, hcc=hcc, cur=cur: e.matmul(
                    pA[:, th * 512:(th + 1) * 512],
                    cur[:, hcc * 128:(hcc + 1) * 128],
                    xn[:, hcc * 1024 + th * 512: hcc * 1024 + (th + 1) * 512],
                    start=(hcc == 0), stop=(hcc == HC - 1), skip_group_check=True),
                    reads=[(ck,), ("xn", hcc), ("pA", th)],
                    writes=[("pA", th)])
        P.ins("scalar", lambda e: e.copy(rope_f[:], pA[:, :]),
              reads=list(A01), writes=[(k_ropef,)])
        for th in range(2):
            P.ins("tensor", lambda e, th=th: e.matmul(
                pB[:, th * 512:(th + 1) * 512], c_rope[:],
                rope_f[:, th * 512:(th + 1) * 512], start=True, stop=True,
                skip_group_check=True),
                reads=[("c_rope",), (k_ropef,)], writes=[("pB", th)])
        P.ins("vector", lambda e: e.tensor_mul(rope_g[:], rope_f[:], c_cos[:]),
              reads=[(k_ropef,), ("c_cos",)], writes=[(k_ropeg,)])
        P.ins("vector", lambda e: e.tensor_mul(rope_f[:], pB[:, :], c_sin[:]),
              reads=list(B01) + [("c_sin",)], writes=[(k_ropef,)])
        P.ins("vector", lambda e, dsb=dsb, dj=dj: e.tensor_add(
            dsb[:, dj * 1024:dj * 1024 + 1024], rope_g[:], rope_f[:]),
            reads=[(k_ropeg,), (k_ropef,)], writes=[(dk, dj)])
        cur, ck, nxt, nk = nxt, nk, cur, ck

    for tc in range(8):
        for hcc in range(HC):
            for ec4 in range(4):
                P.ins("tensor", lambda e, tc=tc, hcc=hcc, ec4=ec4: e.matmul(
                    pA[:, ec4 * 128:(ec4 + 1) * 128],
                    xn[:, hcc * 1024 + tc * 128: hcc * 1024 + tc * 128 + 128],
                    wv_sb[:, hcc * 512 + ec4 * 128: hcc * 512 + (ec4 + 1) * 128],
                    start=(hcc == 0), stop=(hcc == HC - 1), skip_group_check=True),
                    reads=[("xn", hcc), ("wv_sb",), ("pA", 0)],
                    writes=[("pA", 0)])
        for kv in range(NKV):
            P.ins("vector", lambda e, tc=tc, kv=kv: e.tensor_copy(
                vst[:, kv * 520 + tc * 65: kv * 520 + tc * 65 + 64],
                pA[:, kv * 64:(kv + 1) * 64]),
                reads=[("pA", 0)], writes=[("vst", tc)])

    P.ins("gpsimd", lambda e: e.dma_start(
        k_c.ap().rearrange("(ec p) t -> p ec t", p=128),
        kst[:].rearrange("p (ec t) -> p ec t", ec=4)),
        reads=[("kst", i) for i in range(4)], writes=[("k_c",)])
    P.ins("gpsimd", lambda e: e.dma_start(vp_c.ap(), vst[:]),
          reads=[("vst", tc) for tc in range(8)], writes=[("vp_c",)])
    P.col("ag4", lambda e: e.collective_compute(
        "AllGather", ALU.bypass, replica_groups=[[0, 1, 2, 3], [4, 5, 6, 7]],
        ins=[k_c.ap().opt()], outs=[g_k.ap().opt()]),
        reads=[("k_c",)], writes=[("g_k",)])
    P.col("ag5", lambda e: e.collective_compute(
        "AllGather", ALU.bypass, replica_groups=[[0, 1, 2, 3], [4, 5, 6, 7]],
        ins=[vp_c.ap().opt()], outs=[g_vp.ap().opt()]),
        reads=[("vp_c",)], writes=[("g_vp",)])

    ctxC1.close()
    ctxC2 = ExitStack()
    sbC2 = lambda n, s, d: ctxC2.enter_context(nc.sbuf_tensor(n, s, d))
    kb = [sbC2("kb0", [128, 4096], BF16), sbC2("kb1", [128, 4096], BF16)]
    vb = [sbC2("vb0", [128, 4 * 520], BF16), sbC2("vb1", [128, 4 * 520], BF16)]
    pr = [sbC2("pr0", [128, 1024], BF16), sbC2("pr1", [128, 1024], BF16)]
    kbk = ["kb0", "kb1"]
    vbk = ["vb0", "vb1"]
    prk = ["pr0", "pr1"]
    r_sb, k_rsb = st_sum, "st_sum"
    rbc, k_rbc = bc_mrs, "bc_mrs"

    def load_kv_head(kv, ki, vi):
        for rr in range(4):
            P.ins("sync", lambda e, rr=rr, kv=kv, ki=ki: e.dma_start(
                kb[ki][0:64, rr * 1024:(rr + 1) * 1024],
                g_k.ap()[rr * 512 + kv * 64: rr * 512 + (kv + 1) * 64, :]),
                reads=[("g_k",)], writes=[(kbk[ki],)])
            P.ins("sync", lambda e, rr=rr, kv=kv, ki=ki: e.dma_start(
                kb[ki][64:128, rr * 1024:(rr + 1) * 1024],
                g_k.ap()[rr * 512 + kv * 64: rr * 512 + (kv + 1) * 64, :]),
                reads=[("g_k",)], writes=[(kbk[ki],)])
            P.ins("sync", lambda e, rr=rr, kv=kv, vi=vi: e.dma_start(
                vb[vi][:, rr * 520:(rr + 1) * 520],
                g_vp.ap()[rr * 128:(rr + 1) * 128, kv * 520:(kv + 1) * 520]),
                reads=[("g_vp",)], writes=[(vbk[vi],)])

    load_kv_head(0, 0, 0)
    for kv in range(NKV):
        ki = vi = kv % 2
        if kv + 1 < NKV:
            load_kv_head(kv + 1, (kv + 1) % 2, (kv + 1) % 2)
        for g in range(4):
            h = kv * 4 + g
            pb_half = (h % 2) * 64
            ec = h // 2
            for lc in range(32):
                pS, sk = (pA, "pA") if lc % 2 == 0 else (pB, "pB")
                prb, pk = pr[lc % 2], prk[lc % 2]
                for th in range(2):
                    P.ins("tensor", lambda e, lc=lc, th=th, pS=pS, ki=ki,
                          pb_half=pb_half, ec=ec: e.matmul(
                        pS[:, th * 512:(th + 1) * 512],
                        kb[ki][pb_half:pb_half + 64, lc * 128:(lc + 1) * 128],
                        q_sb[pb_half:pb_half + 64,
                             ec * 1024 + th * 512: ec * 1024 + (th + 1) * 512],
                        start=True, stop=True, skip_group_check=True),
                        reads=[(kbk[ki],), ("q_sb", ec)], writes=[(sk, th)])
                P.ins("scalar", lambda e, pS=pS, prb=prb: e.activation(
                    prb[:], pS[:, :], AF.Exp, scale=np.float32(0.125)),
                    reads=[(sk, 0), (sk, 1)], writes=[(pk,)])
                for th in range(2):
                    P.ins("tensor", lambda e, lc=lc, th=th, vi=vi, prb=prb:
                          e.matmul(
                        pC[0:65, th * 512:(th + 1) * 512],
                        vb[vi][:, (lc // 8) * 520 + (lc % 8) * 65:
                               (lc // 8) * 520 + (lc % 8) * 65 + 65],
                        prb[:, th * 512:(th + 1) * 512],
                        start=(lc == 0), stop=(lc == 31), skip_group_check=True),
                        reads=[(vbk[vi],), (pk,), ("pC", th)],
                        writes=[("pC", th)])
            P.ins("vector", lambda e: e.reciprocal(r_sb[:], pC[64:65, :]),
                  reads=list(C01), writes=[(k_rsb,)])
            for th in range(2):
                P.ins("tensor", lambda e, th=th: e.matmul(
                    pB[0:64, 0:512], c_bc64[:],
                    r_sb[0:1, th * 512:(th + 1) * 512], start=True, stop=True,
                    skip_group_check=True),
                    reads=[("c_bc64",), (k_rsb,)], writes=[("pB", 0)])
                P.ins("scalar", lambda e, th=th: e.copy(
                    rbc[0:64, th * 512:(th + 1) * 512], pB[0:64, 0:512]),
                    reads=[("pB", 0)], writes=[(k_rbc,)])
            P.ins("vector", lambda e, ec=ec, pb_half=pb_half: e.tensor_tensor(
                xn[pb_half:pb_half + 64, ec * 1024:(ec + 1) * 1024],
                pC[0:64, :], rbc[0:64, :], ALU.mult),
                reads=list(C01) + [(k_rbc,)], writes=[("xn", ec)])

    cur, ck = wq_b, "wq_b"
    nxt, nk = wq_b2, "wq_b2"
    load_att_chunk(cur, ck, 24)
    for ec in range(16):
        if ec + 1 < 16:
            load_att_chunk(nxt, nk, 24 + ec + 1)
        pW, wk = (pA, "pA") if ec % 2 == 0 else (pB, "pB")
        for th in range(2):
            for hcc in range(HC):
                P.ins("tensor", lambda e, th=th, hcc=hcc, cur=cur, pW=pW: e.matmul(
                    pW[:, th * 512:(th + 1) * 512],
                    cur[:, hcc * 128:(hcc + 1) * 128],
                    xn[:, hcc * 1024 + th * 512: hcc * 1024 + (th + 1) * 512],
                    start=(hcc == 0), stop=(hcc == HC - 1), skip_group_check=True),
                    reads=[(ck,), ("xn", hcc), (wk, th)], writes=[(wk, th)])
        P.ins("vector", lambda e, ec=ec, pW=pW: e.tensor_tensor(
            x_sb[:, ec * 1024:(ec + 1) * 1024], pW[:, :],
            x_sb[:, ec * 1024:(ec + 1) * 1024], ALU.add),
            reads=[(wk, 0), (wk, 1), ("x_sb", ec)], writes=[("x_sb", ec)])
        cur, ck, nxt, nk = nxt, nk, cur, ck
    P.barrier()
    ctxC2.close()
    ctxC.close()

    ctxD = ExitStack()
    sbD = lambda n, s, d: ctxD.enter_context(nc.sbuf_tensor(n, s, d))
    wg2 = sbD("wg2", [128, HC * 512], BF16)
    wu2 = sbD("wu2", [128, HC * 512], BF16)
    wd2 = sbD("wd2", [128, HC * 512], BF16)
    int2 = sbD("int2", [128, 4 * 1024], BF16)
    gl2 = (sbD("glg2", [128, 1024], F32), sbD("gla2", [128, 1024], F32),
           sbD("glb2", [128, 1024], F32), sbD("glc2", [128, 1024], F32))

    emit_ln()
    emit_ffn(wg2, wu2, wd2, int2, gl2,
             ("wg2", "wu2", "wd2", "int2", "glg2", "gla2", "glb2", "glc2"))
    P.barrier()
    ctxD.close()

    ctxE = ExitStack()
    os_ = [ctxE.enter_context(nc.sbuf_tensor("out_stage_a", [128, D], F32)),
           ctxE.enter_context(nc.sbuf_tensor("out_stage_b", [128, D], F32))]
    osk = ["out_stage_a", "out_stage_b"]
    n_to = [0]
    for tc in range(8):
        buf, bk = os_[tc % 2], osk[tc % 2]
        for hc in range(HC):
            pT, pk = (pA, "pA") if n_to[0] % 2 == 0 else (pB, "pB")
            P.ins("tensor", lambda e, tc=tc, hc=hc, pT=pT: e.transpose(
                pT[:, 0:128],
                x_sb[:, hc * 1024 + tc * 128: hc * 1024 + tc * 128 + 128],
                c_id_f[:]),
                reads=[("x_sb", hc), ("c_id_f",)], writes=[(pk, 0)])
            P.ins("vector", lambda e, hc=hc, buf=buf, pT=pT: e.tensor_copy(
                buf[:, hc * 128: hc * 128 + 128], pT[:, 0:128]),
                reads=[(pk, 0)], writes=[(bk,)])
            n_to[0] += 1
        P.ins("gpsimd", lambda e, tc=tc, buf=buf: e.dma_start(
            out.ap()[tc * 128:(tc + 1) * 128, :], buf[:]),
            reads=[(bk,)], writes=[("out",)], stream="out")
    P.final_wait("gpsimd", "out")
    ctxE.close()

    block = ctx.enter_context(nc.Block())

    @block.sync
    def _(sync):
        P.run("sync", sync)

    @block.scalar
    def _(scalar):
        P.run("scalar", scalar)

    @block.vector
    def _(vector):
        P.run("vector", vector)

    @block.tensor
    def _(tensor):
        P.run("tensor", tensor)

    @block.gpsimd
    def _(gpsimd):
        P.run("gpsimd", gpsimd)

    ctx.close()
    return nc


def prepare_in_maps(L0_attn_x, L0_attn_Wq, L0_attn_Wk, L0_attn_Wv, L0_attn_Wo,
                    L0_ffn_gate_W_packed, L0_ffn_gate_scales,
                    L0_ffn_up_W_packed, L0_ffn_up_scales,
                    L0_ffn_down_W_packed, L0_ffn_down_scales):
    x = np.ascontiguousarray(np.asarray(L0_attn_x, np.float32).reshape(8192, D))
    Wq = np.asarray(L0_attn_Wq, np.float32)
    Wk = np.asarray(L0_attn_Wk, np.float32)
    Wv = np.asarray(L0_attn_Wv, np.float32)
    Wo = np.asarray(L0_attn_Wo, np.float32)
    gpk = np.ascontiguousarray(np.asarray(L0_ffn_gate_W_packed, np.int32))
    gscn = np.ascontiguousarray(np.asarray(L0_ffn_gate_scales, np.float32))
    upkk = np.ascontiguousarray(np.asarray(L0_ffn_up_W_packed, np.int32))
    uscn = np.ascontiguousarray(np.asarray(L0_ffn_up_scales, np.float32))
    dpk = np.asarray(L0_ffn_down_W_packed, np.int32)
    dscn = np.asarray(L0_ffn_down_scales, np.float32)

    aw_full = np.ascontiguousarray(np.concatenate([Wq, Wk, Wv, Wo], axis=0))

    half = 32
    inv_freq = 1.0 / (10000.0 ** (np.arange(half, dtype=np.float64) / half))

    in_maps = []
    for c in range(NCORES):
        pos = (c % 4) * TOK + np.arange(TOK, dtype=np.float64)
        ang = pos[None, :] * inv_freq[:, None]
        cs = np.cos(ang).astype(np.float32)
        sn = np.sin(ang).astype(np.float32)
        in_maps.append(dict(
            x_in=x[c * TOK:(c + 1) * TOK],
            gp=gpk[c * 2048:(c + 1) * 2048],
            gsc=gscn[c * 2048:(c + 1) * 2048],
            up_p=upkk[c * 2048:(c + 1) * 2048],
            usc=uscn[c * 2048:(c + 1) * 2048],
            dp=np.ascontiguousarray(dpk[:, c * 1024:(c + 1) * 1024]),
            dsc=np.ascontiguousarray(dscn[:, c * 64:(c + 1) * 64]),
            aw=np.ascontiguousarray(aw_full[c * 640:(c + 1) * 640]),
            cosq=np.ascontiguousarray(np.concatenate([cs, cs, cs, cs], axis=0)),
            sinq=np.ascontiguousarray(np.concatenate([sn, sn, sn, sn], axis=0)),
        ))
    return in_maps


def kernel(**inputs):
    in_maps = prepare_in_maps(**inputs)
    nc = build_program()
    res = run_bass_kernel_spmd(nc, in_maps, core_ids=list(range(NCORES)))
    outs = [np.asarray(res.results[c]["out"], np.float32) for c in range(NCORES)]
    return np.concatenate(outs, axis=0).reshape(2, SEQ, D)


# version 07 (final)
# speedup vs baseline: 1.3423x; 1.1615x over previous
import numpy as np
import ml_dtypes
import concourse.bass as bass
import concourse.mybir as mybir
from concourse.bass_utils import run_bass_kernel_spmd
from contextlib import ExitStack

F32 = mybir.dt.float32
BF16 = mybir.dt.bfloat16
I32 = mybir.dt.int32
I16 = mybir.dt.int16
AF = mybir.ActivationFunctionType
ALU = mybir.AluOpType

NCORES = 8
TOK = 1024
D = 2048
HC = 16
SEQ = 4096
NKV = 8
GELU_B = 0.044715
GELU_A = 0.7978845608028654

CROWS_G = 4 * 2048
CROWS_A = 5 * 512
OFF_WG, OFF_WU, OFF_WD, OFF_AT = 0, CROWS_G, 2 * CROWS_G, 3 * CROWS_G
CROWS = 3 * CROWS_G + CROWS_A


class Prog:
    ENGS = ("sync", "scalar", "vector", "tensor", "gpsimd")
    CENGS = ("scalar", "vector", "tensor")
    INC = {"scalar": 1, "vector": 1, "tensor": 1}

    def __init__(self, mksem):
        self.mksem = mksem
        self.ops = {e: [] for e in self.ENGS}
        self.cnt = {e: 0 for e in self.CENGS}
        self.W = {e: {} for e in self.ENGS}
        self.snap = {}
        self.state = {}
        self.ctr = {e: mksem("ctr_" + e) for e in self.CENGS}
        self.psem = {}
        self.stream = {}
        self.n_waits = 0

    def _deps(self, reads, writes):
        deps = {}
        for k in reads:
            st = self.state.get(k)
            if st is not None and st[0] is not None:
                src, idx = st[0]
                if deps.get(src, 0) < idx:
                    deps[src] = idx
        for k in writes:
            st = self.state.get(k)
            if st is not None:
                if st[0] is not None:
                    src, idx = st[0]
                    if deps.get(src, 0) < idx:
                        deps[src] = idx
                for src, idx in st[1].items():
                    if deps.get(src, 0) < idx:
                        deps[src] = idx
        return deps

    def _emit_waits(self, eng, deps):
        Wm = self.W[eng]
        for src in sorted(deps):
            idx = deps[src]
            if src == "tensor" and eng == "tensor":
                continue
            if src in self.stream:
                idx = self.stream[src][1]
            if Wm.get(src, 0) >= idx:
                continue
            if src in self.psem:
                self.ops[eng].append(
                    lambda e, h=self.psem[src]: e.wait_ge(h, 1))
            elif src in self.stream:
                v = idx * 16
                self.ops[eng].append(
                    lambda e, h=self.stream[src][0], v=v: e.wait_ge(h, v))
            else:
                v = idx * self.INC[src]
                self.ops[eng].append(
                    lambda e, h=self.ctr[src], v=v: e.wait_ge(h, v))
            self.n_waits += 1
            sn = self.snap.get((src, idx))
            if sn:
                for s2, i2 in sn.items():
                    if Wm.get(s2, 0) < i2:
                        Wm[s2] = i2
            if Wm.get(src, 0) < idx:
                Wm[src] = idx

    def _post(self, name, eng, myidx, reads, writes):
        for k in reads:
            st = self.state.get(k)
            if st is None:
                st = self.state[k] = [None, {}]
            st[1][name] = myidx
        for k in writes:
            self.state[k] = [(name, myidx), {}]
        self.snap[(name, myidx)] = dict(self.W[eng])

    def ins(self, eng, fn, reads=(), writes=(), stream=None):
        deps = self._deps(reads, writes)
        self._emit_waits(eng, deps)
        if eng in self.CENGS:
            self.cnt[eng] += 1
            myidx = self.cnt[eng]
            h = self.ctr[eng]
            self.ops[eng].append(lambda e, f=fn, h=h: f(e).then_inc(h, 1))
            self._post(eng, eng, myidx, reads, writes)
        else:
            sname = stream if stream is not None else writes[0][0]
            st = self.stream.get(sname)
            if st is None:
                st = self.stream[sname] = [self.mksem("dma_" + sname), 0]
            st[1] += 1
            myidx = st[1]
            self.ops[eng].append(
                lambda e, f=fn, h=st[0]: f(e).then_inc(h, 16))
            self._post(sname, eng, myidx, reads, writes)

    def col(self, psname, fn, reads=(), writes=()):
        deps = self._deps(reads, writes)
        self._emit_waits("gpsimd", deps)
        h = self.psem[psname]
        self.ops["gpsimd"].append(lambda e, f=fn, h=h: f(e).then_inc(h, 1))
        self._post(psname, "gpsimd", 1, reads, writes)

    def barrier(self):
        targets = [(f, self.cnt[f], self.ctr[f], self.INC[f])
                   for f in self.CENGS if self.cnt[f] > 0]
        targets += [(s, st[1], st[0], 16)
                    for s, st in self.stream.items() if st[1] > 0]
        for e in self.ENGS:
            Wm = self.W[e]
            for f, n, h, inc in targets:
                if f == e or Wm.get(f, 0) >= n:
                    continue
                v = n * inc
                self.ops[e].append(lambda en, h=h, v=v: en.wait_ge(h, v))
                Wm[f] = n
                self.n_waits += 1

    def final_wait(self, eng, sname):
        st = self.stream[sname]
        v = st[1] * 16
        self.ops[eng].append(lambda e, h=st[0], v=v: e.wait_ge(h, v))

    def run(self, eng_name, e):
        for f in self.ops[eng_name]:
            f(e)


def build_program():
    nc = bass.Bass("TRN2", target_bir_lowering=False)

    x_in = nc.dram_tensor("x_in", [TOK, D], F32, kind="ExternalInput")
    gp = nc.dram_tensor("gp", [2048, 1024], I32, kind="ExternalInput")
    gsc = nc.dram_tensor("gsc", [2048, 64], F32, kind="ExternalInput")
    up_p = nc.dram_tensor("up_p", [2048, 1024], I32, kind="ExternalInput")
    usc = nc.dram_tensor("usc", [2048, 64], F32, kind="ExternalInput")
    dp = nc.dram_tensor("dp", [2048, 1024], I32, kind="ExternalInput")
    dsc = nc.dram_tensor("dsc", [2048, 64], F32, kind="ExternalInput")
    aw = nc.dram_tensor("aw", [640, D], F32, kind="ExternalInput")
    cosq = nc.dram_tensor("cosq", [128, TOK], F32, kind="ExternalInput")
    sinq = nc.dram_tensor("sinq", [128, TOK], F32, kind="ExternalInput")
    out = nc.dram_tensor("out", [TOK, D], F32, kind="ExternalOutput")

    contrib = nc.dram_tensor("contrib", [CROWS, 512], BF16)
    g_w = nc.dram_tensor("g_w", [NCORES * CROWS, 512], BF16,
                         addr_space="Shared")
    k_c = nc.dram_tensor("k_c", [512, TOK], BF16)
    vp_c = nc.dram_tensor("vp_c", [128, NKV * 8 * 65], BF16)
    g_k = nc.dram_tensor("g_k", [4 * 512, TOK], BF16)
    g_vp = nc.dram_tensor("g_vp", [4 * 128, NKV * 8 * 65], BF16)

    def gw_off(mat, r, b):
        return (mat * NCORES * CROWS_G + r * CROWS_G + b * 2048) * 512

    def gwd_base(r):
        return (2 * NCORES * CROWS_G + r * CROWS_G) * 512

    def gatt_off(i):
        r, j = i // 5, i % 5
        return (3 * NCORES * CROWS_G + r * CROWS_A + j * 512) * 512

    identF = nc.inline_tensor(np.eye(128, dtype=np.float32), "identF")
    identB = nc.inline_tensor(np.eye(128).astype(ml_dtypes.bfloat16), "identB")
    ropeT_np = np.zeros((128, 128), np.float32)
    for b in range(2):
        for i in range(32):
            ropeT_np[b * 64 + i + 32, b * 64 + i] = -1.0
            ropeT_np[b * 64 + i, b * 64 + i + 32] = 1.0
    ropeT = nc.inline_tensor(ropeT_np, "ropeT")
    ones128 = nc.inline_tensor(np.ones((128, 1), np.float32), "ones128")
    bc128 = nc.inline_tensor(np.ones((1, 128), np.float32), "bc128")
    bc64 = nc.inline_tensor(np.ones((1, 64), np.float32), "bc64")

    ctx = ExitStack()
    sb = lambda n, s, d: ctx.enter_context(nc.sbuf_tensor(n, s, d))
    ps = lambda n, s, d: ctx.enter_context(nc.psum_tensor(n, s, d))
    mksem = lambda n: ctx.enter_context(nc.semaphore(n))

    P = Prog(mksem)
    for a in range(6):
        P.psem["ag%d" % a] = mksem("ag%d" % a)

    pA = ps("pA", [128, 1024], F32)
    pB = ps("pB", [128, 1024], F32)
    pC = ps("pC", [128, 1024], F32)
    pEa = ps("pEa", [128, 512], BF16)
    pEb = ps("pEb", [128, 512], BF16)
    pE = [pEa, pEb]
    pEk = ["pEa", "pEb"]
    A01 = (("pA", 0), ("pA", 1))
    B01 = (("pB", 0), ("pB", 1))
    C01 = (("pC", 0), ("pC", 1))

    x_sb = sb("x_sb", [128, HC * 1024], F32)
    c_id_f = sb("c_id_f", [128, 128], F32)
    c_id_b = sb("c_id_b", [128, 128], BF16)
    c_rope = sb("c_rope", [128, 128], F32)
    c_ones = sb("c_ones", [128, 1], F32)
    c_bc128 = sb("c_bc128", [1, 128], F32)
    c_bc64 = sb("c_bc64", [1, 64], F32)
    c_cos = sb("c_cos", [128, TOK], F32)
    c_sin = sb("c_sin", [128, TOK], F32)
    xn = sb("xn", [128, HC * 1024], BF16)
    st_sum = sb("st_sum", [1, 1024], F32)
    st_sq = sb("st_sq", [1, 1024], F32)
    st_rstd = sb("st_rstd", [1, 1024], F32)
    st_mrs = sb("st_mrs", [1, 1024], F32)
    bc_rstd = sb("bc_rstd", [128, 1024], F32)
    bc_mrs = sb("bc_mrs", [128, 1024], F32)
    ln_tmp = sb("ln_tmp", [128, 1024], F32)

    for dst, src, key in [
            (c_id_f, identF, "c_id_f"), (c_id_b, identB, "c_id_b"),
            (c_rope, ropeT, "c_rope"), (c_ones, ones128, "c_ones"),
            (c_bc128, bc128, "c_bc128"), (c_bc64, bc64, "c_bc64"),
            (c_cos, cosq, "c_cos"), (c_sin, sinq, "c_sin")]:
        P.ins("sync", lambda e, d=dst, s=src: e.dma_start(d[:], s.ap()),
              writes=[(key,)], stream="consts")

    def emit_ln():
        for th in range(2):
            for hc in range(HC):
                P.ins("tensor", lambda e, hc=hc, th=th: e.matmul(
                    pA[0:1, th * 512:(th + 1) * 512], c_ones[:],
                    x_sb[:, hc * 1024 + th * 512: hc * 1024 + (th + 1) * 512],
                    start=(hc == 0), stop=(hc == HC - 1), skip_group_check=True),
                    reads=[("c_ones",), ("x_sb", hc), ("pA", th)],
                    writes=[("pA", th)])
        for hc in range(HC):
            P.ins("scalar", lambda e, hc=hc: e.activation(
                ln_tmp[:], x_sb[:, hc * 1024:(hc + 1) * 1024], AF.Square),
                reads=[("x_sb", hc)], writes=[("ln_tmp",)])
            for th in range(2):
                P.ins("tensor", lambda e, hc=hc, th=th: e.matmul(
                    pB[0:1, th * 512:(th + 1) * 512], c_ones[:],
                    ln_tmp[:, th * 512:(th + 1) * 512],
                    start=(hc == 0), stop=(hc == HC - 1), skip_group_check=True),
                    reads=[("c_ones",), ("ln_tmp",), ("pB", th)],
                    writes=[("pB", th)])
        P.ins("vector", lambda e: e.tensor_single_scalar(
            st_sum[:], pA[0:1, :], 1.0 / D, ALU.mult),
            reads=list(A01), writes=[("st_sum",)])
        P.ins("vector", lambda e: e.tensor_single_scalar(
            st_sq[:], pB[0:1, :], 1.0 / D, ALU.mult),
            reads=list(B01), writes=[("st_sq",)])
        P.ins("vector", lambda e: e.tensor_mul(st_rstd[:], st_sum[:], st_sum[:]),
              reads=[("st_sum",)], writes=[("st_rstd",)])
        P.ins("vector", lambda e: e.tensor_sub(st_rstd[:], st_sq[:], st_rstd[:]),
              reads=[("st_sq",), ("st_rstd",)], writes=[("st_rstd",)])
        P.ins("scalar", lambda e: e.activation(st_rstd[:], st_rstd[:], AF.Sqrt,
                                               bias=np.float32(1e-5)),
              reads=[("st_rstd",)], writes=[("st_rstd",)])
        P.ins("vector", lambda e: e.reciprocal(st_rstd[:], st_rstd[:]),
              reads=[("st_rstd",)], writes=[("st_rstd",)])
        P.ins("vector", lambda e: e.tensor_mul(st_mrs[:], st_sum[:], st_rstd[:]),
              reads=[("st_sum",), ("st_rstd",)], writes=[("st_mrs",)])
        for th in range(2):
            P.ins("tensor", lambda e, th=th: e.matmul(
                pA[:, th * 512:(th + 1) * 512], c_bc128[:],
                st_rstd[0:1, th * 512:(th + 1) * 512], start=True, stop=True,
                skip_group_check=True),
                reads=[("c_bc128",), ("st_rstd",)], writes=[("pA", th)])
            P.ins("tensor", lambda e, th=th: e.matmul(
                pB[:, th * 512:(th + 1) * 512], c_bc128[:],
                st_mrs[0:1, th * 512:(th + 1) * 512], start=True, stop=True,
                skip_group_check=True),
                reads=[("c_bc128",), ("st_mrs",)], writes=[("pB", th)])
        P.ins("scalar", lambda e: e.copy(bc_rstd[:], pA[:, :]),
              reads=list(A01), writes=[("bc_rstd",)])
        P.ins("scalar", lambda e: e.copy(bc_mrs[:], pB[:, :]),
              reads=list(B01), writes=[("bc_mrs",)])
        for hc in range(HC):
            P.ins("vector", lambda e, hc=hc: e.tensor_tensor(
                ln_tmp[:], x_sb[:, hc * 1024:(hc + 1) * 1024], bc_rstd[:],
                ALU.mult),
                reads=[("x_sb", hc), ("bc_rstd",)], writes=[("ln_tmp",)])
            P.ins("vector", lambda e, hc=hc: e.tensor_tensor(
                xn[:, hc * 1024:(hc + 1) * 1024], ln_tmp[:], bc_mrs[:],
                ALU.subtract),
                reads=[("ln_tmp",), ("bc_mrs",)], writes=[("xn", hc)])

    def emit_ffn(wg_b, wu_b, wd_b, inter, gl, keys):
        gl_g, gl_a, gl_b, gl_c = gl
        kwg, kwu, kwd, kint, kg, ka, kb_, kc = keys
        for s in range(32):
            r, b = s // 4, s % 4
            P.ins("sync", lambda e, r=r, b=b: e.dma_start(
                wg_b[:], bass.AP(g_w, gw_off(0, r, b), [[8192, 128], [1, 8192]])),
                reads=[("g_w", "g")], writes=[(kwg,)])
            P.ins("sync", lambda e, r=r, b=b: e.dma_start(
                wu_b[:], bass.AP(g_w, gw_off(1, r, b), [[8192, 128], [1, 8192]])),
                reads=[("g_w", "u")], writes=[(kwu,)])
            P.ins("sync", lambda e, r=r, b=b: e.dma_start(
                wd_b[:].rearrange("p (cc n) -> p cc n", cc=16),
                bass.AP(g_w, gwd_base(r) + 512 * b,
                        [[2048, 128], [262144, 16], [1, 512]])),
                reads=[("g_w", "d")], writes=[(kwd,)])
            for nn in range(4):
                for hcc in range(HC):
                    for th in range(2):
                        P.ins("tensor", lambda e, nn=nn, th=th, hcc=hcc: e.matmul(
                            pA[:, th * 512:(th + 1) * 512],
                            wg_b[:, hcc * 512 + nn * 128: hcc * 512 + nn * 128 + 128],
                            xn[:, hcc * 1024 + th * 512: hcc * 1024 + (th + 1) * 512],
                            start=(hcc == 0), stop=(hcc == HC - 1),
                            skip_group_check=True),
                            reads=[(kwg,), ("xn", hcc), ("pA", th)],
                            writes=[("pA", th)])
                for hcc in range(HC):
                    for th in range(2):
                        P.ins("tensor", lambda e, nn=nn, th=th, hcc=hcc: e.matmul(
                            pB[:, th * 512:(th + 1) * 512],
                            wu_b[:, hcc * 512 + nn * 128: hcc * 512 + nn * 128 + 128],
                            xn[:, hcc * 1024 + th * 512: hcc * 1024 + (th + 1) * 512],
                            start=(hcc == 0), stop=(hcc == HC - 1),
                            skip_group_check=True),
                            reads=[(kwu,), ("xn", hcc), ("pB", th)],
                            writes=[("pB", th)])
                P.ins("scalar", lambda e: e.copy(gl_g[:], pA[:, :]),
                      reads=list(A01), writes=[(kg,)])
                P.ins("scalar", lambda e: e.activation(gl_a[:], gl_g[:], AF.Square),
                      reads=[(kg,)], writes=[(ka,)])
                P.ins("vector", lambda e: e.tensor_scalar(
                    gl_b[:], gl_a[:], GELU_B, 1.0, ALU.mult, ALU.add),
                    reads=[(ka,)], writes=[(kb_,)])
                P.ins("vector", lambda e: e.tensor_mul(gl_a[:], gl_g[:], gl_b[:]),
                      reads=[(kg,), (kb_,)], writes=[(ka,)])
                P.ins("scalar", lambda e: e.activation(
                    gl_b[:], gl_a[:], AF.Tanh, scale=np.float32(GELU_A)),
                    reads=[(ka,)], writes=[(kb_,)])
                P.ins("scalar", lambda e: e.copy(gl_c[:], pB[:, :]),
                      reads=list(B01), writes=[(kc,)])
                P.ins("vector", lambda e: e.tensor_mul(gl_a[:], gl_g[:], gl_b[:]),
                      reads=[(kg,), (kb_,)], writes=[(ka,)])
                P.ins("vector", lambda e: e.tensor_add(gl_b[:], gl_a[:], gl_g[:]),
                      reads=[(ka,), (kg,)], writes=[(kb_,)])
                P.ins("vector", lambda e, nn=nn: e.tensor_mul(
                    inter[:, nn * 1024:(nn + 1) * 1024], gl_b[:], gl_c[:]),
                    reads=[(kb_,), (kc,)], writes=[(kint, nn)])
            for hcc in range(HC):
                pD, D01 = (pC, C01) if hcc % 2 == 0 else (pB, B01)
                for nn in range(4):
                    for th in range(2):
                        P.ins("tensor", lambda e, hcc=hcc, th=th, nn=nn, pD=pD:
                              e.matmul(
                            pD[:, th * 512:(th + 1) * 512],
                            wd_b[:, hcc * 512 + nn * 128: hcc * 512 + nn * 128 + 128],
                            inter[:, nn * 1024 + th * 512: nn * 1024 + (th + 1) * 512],
                            start=(nn == 0), stop=(nn == 3), skip_group_check=True),
                            reads=[(kwd,), (kint, nn), D01[th]],
                            writes=[D01[th]])
                P.ins("vector", lambda e, hcc=hcc, pD=pD: e.scalar_tensor_tensor(
                    x_sb[:, hcc * 1024:(hcc + 1) * 1024], pD[:, :], 0.25,
                    x_sb[:, hcc * 1024:(hcc + 1) * 1024], ALU.mult, ALU.add),
                    reads=list(D01) + [("x_sb", hcc)], writes=[("x_sb", hcc)])

    ctxX = ExitStack()
    xs = [ctxX.enter_context(nc.sbuf_tensor("x_stage_a", [128, D], F32)),
          ctxX.enter_context(nc.sbuf_tensor("x_stage_b", [128, D], F32))]
    xsk = ["x_stage_a", "x_stage_b"]

    n_tr = [0]
    for tc in range(8):
        buf, bk = xs[tc % 2], xsk[tc % 2]
        P.ins("sync", lambda e, tc=tc, buf=buf: e.dma_start(
            buf[:], x_in.ap()[tc * 128:(tc + 1) * 128, :]),
            writes=[(bk,)])
        for hc in range(HC):
            pT, pk = (pA, "pA") if n_tr[0] % 2 == 0 else (pB, "pB")
            P.ins("tensor", lambda e, hc=hc, buf=buf, pT=pT: e.transpose(
                pT[:, 0:128], buf[:, hc * 128: hc * 128 + 128], c_id_f[:]),
                reads=[(bk,), ("c_id_f",)], writes=[(pk, 0)])
            P.ins("vector", lambda e, hc=hc, tc=tc, pT=pT: e.tensor_copy(
                x_sb[:, hc * 1024 + tc * 128: hc * 1024 + tc * 128 + 128],
                pT[:, 0:128]),
                reads=[(pk, 0)], writes=[("x_sb", hc)])
            n_tr[0] += 1

    emit_ln()

    ctxX.close()
    ctxA = ExitStack()
    sbA = lambda n, s, d: ctxA.enter_context(nc.sbuf_tensor(n, s, d))
    dqset = []
    for sfx in ("a", "b"):
        dqset.append(dict(
            pk32=sbA("dq_pk32" + sfx, [128, 1024], I32),
            p16=sbA("dq_p16" + sfx, [128, 1024], I16),
            m=sbA("dq_m" + sfx, [128, 1024], I16),
            sg=sbA("dq_sg" + sfx, [128, 1024], I16),
            u=sbA("dq_u" + sfx, [128, 1024], BF16),
            r=sbA("dq_r" + sfx, [128, 1024], BF16),
            w=sbA("dq_w" + sfx, [128, 1024], BF16),
            w2=sbA("dq_w2" + sfx, [128, 1024], BF16),
            sc=sbA("dq_sc" + sfx, [128, 64], F32),
            wnat=sbA("wnat" + sfx, [128, 2048], BF16),
            sfx=sfx,
        ))
    c_sb = sbA("c_sb", [128, HC * 512], BF16)
    aw_sb = sbA("aw_sb", [128, D], F32)
    aw_b = sbA("aw_b", [128, D], BF16)

    n_dq = [0]
    n_trw = [0]

    def emit_dq_chunk(pk_t, sc_t, cc, first=False):
        q = dqset[n_dq[0] % 2]
        n_dq[0] += 1
        s = q["sfx"]
        extra = [(k,) for k in xsk] if first else []
        P.ins("gpsimd", lambda e, cc=cc, pk_t=pk_t, q=q: e.dma_start(
            q["pk32"][:], pk_t.ap()[cc * 128:(cc + 1) * 128, :]),
            writes=[("dq_pk32" + s,)] + extra, stream="dq_pk32" + s)
        P.ins("gpsimd", lambda e, cc=cc, sc_t=sc_t, q=q: e.dma_start(
            q["sc"][:], sc_t.ap()[cc * 128:(cc + 1) * 128, :]),
            writes=[("dq_sc" + s,)])
        P.ins("vector", lambda e, q=q: e.tensor_copy(q["p16"][:], q["pk32"][:]),
              reads=[("dq_pk32" + s,)], writes=[("dq_p16" + s,)])
        for half in range(2):
            if half == 0:
                P.ins("vector", lambda e, q=q: e.tensor_scalar(
                    q["m"][:], q["p16"][:], 7, 6,
                    ALU.bitwise_and, ALU.logical_shift_left),
                    reads=[("dq_p16" + s,)], writes=[("dq_m" + s,)])
                P.ins("vector", lambda e, q=q: e.tensor_scalar(
                    q["sg"][:], q["p16"][:], 8, 12,
                    ALU.bitwise_and, ALU.logical_shift_left),
                    reads=[("dq_p16" + s,)], writes=[("dq_sg" + s,)])
            else:
                P.ins("vector", lambda e, q=q: e.tensor_scalar(
                    q["m"][:], q["p16"][:], 2, 0x1C0,
                    ALU.logical_shift_left, ALU.bitwise_and),
                    reads=[("dq_p16" + s,)], writes=[("dq_m" + s,)])
                P.ins("vector", lambda e, q=q: e.tensor_scalar(
                    q["sg"][:], q["p16"][:], 128, 8,
                    ALU.bitwise_and, ALU.logical_shift_left),
                    reads=[("dq_p16" + s,)], writes=[("dq_sg" + s,)])
            P.ins("vector", lambda e, q=q: e.tensor_single_scalar(
                q["u"][:].bitcast(I16), q["m"][:], 0x3F00, ALU.add),
                reads=[("dq_m" + s,)], writes=[("dq_u" + s,)])
            P.ins("scalar", lambda e, q=q: e.activation(
                q["r"][:], q["u"][:], AF.Relu, bias=np.float32(1.0),
                scale=np.float32(-1.0)),
                reads=[("dq_u" + s,)], writes=[("dq_r" + s,)])
            wk = ("dq_w" if half == 0 else "dq_w2") + s
            wt = q["w"] if half == 0 else q["w2"]
            P.ins("vector", lambda e, wt=wt, q=q: e.tensor_sub(
                wt[:], q["u"][:], q["r"][:]),
                reads=[("dq_u" + s,), ("dq_r" + s,)], writes=[(wk,)])
            P.ins("vector", lambda e, wt=wt, q=q: e.tensor_add(
                wt[:].bitcast(I16), wt[:].bitcast(I16), q["sg"][:]),
                reads=[(wk,), ("dq_sg" + s,)], writes=[(wk,)])
        sc_bcast = bass.AP(q["sc"], 0, [[64, 128], [1, 64], [0, 16]])
        P.ins("vector", lambda e, q=q, sc_bcast=sc_bcast: e.tensor_tensor(
            q["wnat"][:, 0:2048:2], q["w"][:], sc_bcast, ALU.mult),
            reads=[("dq_w" + s,), ("dq_sc" + s,)], writes=[("wnat" + s,)])
        P.ins("vector", lambda e, q=q, sc_bcast=sc_bcast: e.tensor_tensor(
            q["wnat"][:, 1:2048:2], q["w2"][:], sc_bcast, ALU.mult),
            reads=[("dq_w2" + s,), ("dq_sc" + s,)], writes=[("wnat" + s,)])
        return q

    def emit_transposes(src_sb, src_key, dst4_fn):
        for grp in range(4):
            pT, pk = pE[n_trw[0] % 2], pEk[n_trw[0] % 2]
            for q4 in range(4):
                hh = grp * 4 + q4
                P.ins("tensor", lambda e, hh=hh, q4=q4, pT=pT, src=src_sb:
                      e.transpose(
                    pT[:, q4 * 128:(q4 + 1) * 128],
                    src[:, hh * 128:(hh + 1) * 128], c_id_b[:]),
                    reads=[(src_key,), ("c_id_b",)], writes=[(pk,)])
            P.ins("scalar", lambda e, grp=grp, pT=pT, f=dst4_fn: e.copy(
                f(grp), pT[:, 0:512]),
                reads=[(pk,)], writes=[("c_sb",)])
            n_trw[0] += 1

    first_dq = [True]
    for mat, (pk_t, sc_t, off, gk) in enumerate(
            [(gp, gsc, OFF_WG, "g"), (up_p, usc, OFF_WU, "u")]):
        for cc in range(16):
            q = emit_dq_chunk(pk_t, sc_t, cc, first=first_dq[0])
            first_dq[0] = False
            emit_transposes(q["wnat"], "wnat" + q["sfx"],
                            lambda grp, cc=cc: bass.AP(
                                c_sb, grp * 4 * 512 + (cc % 4) * 128,
                                [[8192, 128], [512, 4], [1, 128]]))
            if cc % 4 == 3:
                b = cc // 4
                P.ins("gpsimd", lambda e, off=off, b=b: e.dma_start(
                    bass.AP(contrib, (off + b * 2048) * 512,
                            [[8192, 128], [1, 8192]]),
                    c_sb[:]),
                    reads=[("c_sb",)], writes=[("contrib", gk, b)],
                    stream="contrib")
        P.col("ag%d" % mat, lambda e, off=off, mat=mat: e.collective_compute(
            "AllGather", ALU.bypass, replica_groups=[list(range(8))],
            ins=[contrib.ap()[off:off + CROWS_G, :].opt()],
            outs=[bass.AP(g_w, mat * NCORES * CROWS_G * 512,
                          [[512, NCORES * CROWS_G], [1, 512]])],
        ), reads=[("contrib", gk, b) for b in range(4)], writes=[("g_w", gk)])

    for cc in range(16):
        q = emit_dq_chunk(dp, dsc, cc)
        emit_transposes(q["wnat"], "wnat" + q["sfx"],
                        lambda grp: c_sb[:, grp * 512:(grp + 1) * 512])
        P.ins("gpsimd", lambda e, cc=cc: e.dma_start(
            bass.AP(contrib, (OFF_WD + cc * 512) * 512, [[2048, 128], [1, 2048]]),
            c_sb[:, 0:2048]),
            reads=[("c_sb",)], writes=[("contrib", "d", cc)], stream="contrib")
    P.col("ag2", lambda e: e.collective_compute(
        "AllGather", ALU.bypass, replica_groups=[list(range(8))],
        ins=[contrib.ap()[OFF_WD:OFF_WD + CROWS_G, :].opt()],
        outs=[bass.AP(g_w, 2 * NCORES * CROWS_G * 512,
                      [[512, NCORES * CROWS_G], [1, 512]])],
    ), reads=[("contrib", "d", cc) for cc in range(16)], writes=[("g_w", "d")])

    P.ins("sync", lambda e: e.dma_start(aw_sb[:], aw.ap()[0:128, :]),
          writes=[("aw_sb",), ("x_stage_a",), ("x_stage_b",)], stream="aw_sb")
    for j in range(5):
        P.ins("vector", lambda e: e.tensor_copy(aw_b[:], aw_sb[:]),
              reads=[("aw_sb",)], writes=[("aw_b",)])
        if j < 4:
            P.ins("sync", lambda e, j=j: e.dma_start(
                aw_sb[:], aw.ap()[(j + 1) * 128:(j + 2) * 128, :]),
                writes=[("aw_sb",)], stream="aw_sb")
        emit_transposes(aw_b, "aw_b",
                        lambda grp: c_sb[:, grp * 512:(grp + 1) * 512])
        P.ins("gpsimd", lambda e, j=j: e.dma_start(
            bass.AP(contrib, (OFF_AT + j * 512) * 512, [[2048, 128], [1, 2048]]),
            c_sb[:, 0:2048]),
            reads=[("c_sb",)], writes=[("contrib", "a", j)], stream="contrib")
    P.col("ag3", lambda e: e.collective_compute(
        "AllGather", ALU.bypass, replica_groups=[list(range(8))],
        ins=[contrib.ap()[OFF_AT:OFF_AT + CROWS_A, :].opt()],
        outs=[bass.AP(g_w, 3 * NCORES * CROWS_G * 512,
                      [[512, NCORES * CROWS_A], [1, 512]])],
    ), reads=[("contrib", "a", j) for j in range(5)], writes=[("g_w", "a")])

    P.barrier()
    ctxA.close()

    ctxB = ExitStack()
    sbB = lambda n, s, d: ctxB.enter_context(nc.sbuf_tensor(n, s, d))
    wg1 = sbB("wg1", [128, HC * 512], BF16)
    wu1 = sbB("wu1", [128, HC * 512], BF16)
    wd1 = sbB("wd1", [128, HC * 512], BF16)
    int1 = sbB("int1", [128, 4 * 1024], BF16)
    gl1 = (sbB("glg1", [128, 1024], F32), sbB("gla1", [128, 1024], F32),
           sbB("glb1", [128, 1024], F32), sbB("glc1", [128, 1024], F32))

    emit_ffn(wg1, wu1, wd1, int1, gl1,
             ("wg1", "wu1", "wd1", "int1", "glg1", "gla1", "glb1", "glc1"))
    P.barrier()
    ctxB.close()

    ctxC = ExitStack()
    sbC = lambda n, s, d: ctxC.enter_context(nc.sbuf_tensor(n, s, d))
    q_sb = sbC("q_sb", [128, HC * 1024], BF16)
    wq_b = sbC("wq_b", [128, HC * 128], BF16)
    wq_b2 = sbC("wq_b2", [128, HC * 128], BF16)
    rope_f, k_ropef = ln_tmp, "ln_tmp"
    rope_g, k_ropeg = bc_rstd, "bc_rstd"

    emit_ln()

    ctxC1 = ExitStack()
    sbC1 = lambda n, s, d: ctxC1.enter_context(nc.sbuf_tensor(n, s, d))
    kst = sbC1("kst", [128, 4 * 1024], BF16)
    vst = sbC1("vst", [128, NKV * 8 * 65], BF16)
    wv_sb = sbC1("wv_sb", [128, HC * 512], BF16)

    ones_ap = bass.AP(vst, 64, [[NKV * 8 * 65, 128], [520, NKV], [65, 8], [1, 1]])
    P.ins("vector", lambda e: e.memset(ones_ap, 1.0),
          writes=[("vst", tc) for tc in range(8)])

    for ec4 in range(4):
        P.ins("sync", lambda e, ec4=ec4: e.dma_start(
            bass.AP(wv_sb, ec4 * 128, [[HC * 512, 128], [512, 16], [1, 128]]),
            bass.AP(g_w, gatt_off(20 + ec4), [[2048, 128], [128, 16], [1, 128]])),
            reads=[("g_w", "a")], writes=[("wv_sb",)])

    def load_att_chunk(buf, bk, i):
        P.ins("sync", lambda e, buf=buf, i=i: e.dma_start(
            buf[:], bass.AP(g_w, gatt_off(i), [[2048, 128], [1, 2048]])),
            reads=[("g_w", "a")], writes=[(bk,)])

    qk_list = [(i, q_sb, "q_sb", i) for i in range(16)] + \
              [(16 + i, kst, "kst", i) for i in range(4)]
    cur, ck = wq_b, "wq_b"
    nxt, nk = wq_b2, "wq_b2"
    load_att_chunk(cur, ck, qk_list[0][0])
    for idx, (ci, dsb, dk, dj) in enumerate(qk_list):
        if idx + 1 < len(qk_list):
            load_att_chunk(nxt, nk, qk_list[idx + 1][0])
        for hcc in range(HC):
            for th in range(2):
                P.ins("tensor", lambda e, th=th, hcc=hcc, cur=cur: e.matmul(
                    pA[:, th * 512:(th + 1) * 512],
                    cur[:, hcc * 128:(hcc + 1) * 128],
                    xn[:, hcc * 1024 + th * 512: hcc * 1024 + (th + 1) * 512],
                    start=(hcc == 0), stop=(hcc == HC - 1), skip_group_check=True),
                    reads=[(ck,), ("xn", hcc), ("pA", th)],
                    writes=[("pA", th)])
        P.ins("scalar", lambda e: e.copy(rope_f[:], pA[:, :]),
              reads=list(A01), writes=[(k_ropef,)])
        for th in range(2):
            P.ins("tensor", lambda e, th=th: e.matmul(
                pB[:, th * 512:(th + 1) * 512], c_rope[:],
                rope_f[:, th * 512:(th + 1) * 512], start=True, stop=True,
                skip_group_check=True),
                reads=[("c_rope",), (k_ropef,)], writes=[("pB", th)])
        P.ins("vector", lambda e: e.tensor_mul(rope_g[:], rope_f[:], c_cos[:]),
              reads=[(k_ropef,), ("c_cos",)], writes=[(k_ropeg,)])
        P.ins("vector", lambda e: e.tensor_mul(rope_f[:], pB[:, :], c_sin[:]),
              reads=list(B01) + [("c_sin",)], writes=[(k_ropef,)])
        P.ins("vector", lambda e, dsb=dsb, dj=dj: e.tensor_add(
            dsb[:, dj * 1024:dj * 1024 + 1024], rope_g[:], rope_f[:]),
            reads=[(k_ropeg,), (k_ropef,)], writes=[(dk, dj)])
        cur, ck, nxt, nk = nxt, nk, cur, ck

    for tc in range(8):
        for hcc in range(HC):
            for ec4 in range(4):
                P.ins("tensor", lambda e, tc=tc, hcc=hcc, ec4=ec4: e.matmul(
                    pA[:, ec4 * 128:(ec4 + 1) * 128],
                    xn[:, hcc * 1024 + tc * 128: hcc * 1024 + tc * 128 + 128],
                    wv_sb[:, hcc * 512 + ec4 * 128: hcc * 512 + (ec4 + 1) * 128],
                    start=(hcc == 0), stop=(hcc == HC - 1), skip_group_check=True),
                    reads=[("xn", hcc), ("wv_sb",), ("pA", 0)],
                    writes=[("pA", 0)])
        for kv in range(NKV):
            P.ins("vector", lambda e, tc=tc, kv=kv: e.tensor_copy(
                vst[:, kv * 520 + tc * 65: kv * 520 + tc * 65 + 64],
                pA[:, kv * 64:(kv + 1) * 64]),
                reads=[("pA", 0)], writes=[("vst", tc)])

    P.ins("gpsimd", lambda e: e.dma_start(
        k_c.ap().rearrange("(ec p) t -> p ec t", p=128),
        kst[:].rearrange("p (ec t) -> p ec t", ec=4)),
        reads=[("kst", i) for i in range(4)], writes=[("k_c",)])
    P.ins("gpsimd", lambda e: e.dma_start(vp_c.ap(), vst[:]),
          reads=[("vst", tc) for tc in range(8)], writes=[("vp_c",)])
    P.col("ag4", lambda e: e.collective_compute(
        "AllGather", ALU.bypass, replica_groups=[[0, 1, 2, 3], [4, 5, 6, 7]],
        ins=[k_c.ap().opt()], outs=[g_k.ap().opt()]),
        reads=[("k_c",)], writes=[("g_k",)])
    P.col("ag5", lambda e: e.collective_compute(
        "AllGather", ALU.bypass, replica_groups=[[0, 1, 2, 3], [4, 5, 6, 7]],
        ins=[vp_c.ap().opt()], outs=[g_vp.ap().opt()]),
        reads=[("vp_c",)], writes=[("g_vp",)])

    ctxC1.close()
    ctxC2 = ExitStack()
    sbC2 = lambda n, s, d: ctxC2.enter_context(nc.sbuf_tensor(n, s, d))
    kb = [sbC2("kb0", [128, 4096], BF16), sbC2("kb1", [128, 4096], BF16)]
    vb = [sbC2("vb0", [128, 4 * 520], BF16), sbC2("vb1", [128, 4 * 520], BF16)]
    pr = [sbC2("pr0", [128, 1024], BF16), sbC2("pr1", [128, 1024], BF16)]
    kbk = ["kb0", "kb1"]
    vbk = ["vb0", "vb1"]
    prk = ["pr0", "pr1"]
    r_sb, k_rsb = st_sum, "st_sum"
    rbc, k_rbc = bc_mrs, "bc_mrs"

    def load_kv_head(kv, ki, vi):
        for rr in range(4):
            P.ins("sync", lambda e, rr=rr, kv=kv, ki=ki: e.dma_start(
                kb[ki][0:64, rr * 1024:(rr + 1) * 1024],
                g_k.ap()[rr * 512 + kv * 64: rr * 512 + (kv + 1) * 64, :]),
                reads=[("g_k",)], writes=[(kbk[ki],)])
            P.ins("sync", lambda e, rr=rr, kv=kv, ki=ki: e.dma_start(
                kb[ki][64:128, rr * 1024:(rr + 1) * 1024],
                g_k.ap()[rr * 512 + kv * 64: rr * 512 + (kv + 1) * 64, :]),
                reads=[("g_k",)], writes=[(kbk[ki],)])
            P.ins("sync", lambda e, rr=rr, kv=kv, vi=vi: e.dma_start(
                vb[vi][:, rr * 520:(rr + 1) * 520],
                g_vp.ap()[rr * 128:(rr + 1) * 128, kv * 520:(kv + 1) * 520]),
                reads=[("g_vp",)], writes=[(vbk[vi],)])

    avs, k_avs = bc_rstd, "bc_rstd"

    def make_tail(ec_, ph_):
        def tail():
            P.ins("vector", lambda e: e.reciprocal(r_sb[:], avs[64:65, :]),
                  reads=[(k_avs,)], writes=[(k_rsb,)])
            for th in range(2):
                P.ins("tensor", lambda e, th=th: e.matmul(
                    pB[0:64, 0:512], c_bc64[:],
                    r_sb[0:1, th * 512:(th + 1) * 512], start=True, stop=True,
                    skip_group_check=True),
                    reads=[("c_bc64",), (k_rsb,)], writes=[("pB", 0)])
                P.ins("scalar", lambda e, th=th: e.copy(
                    rbc[0:64, th * 512:(th + 1) * 512], pB[0:64, 0:512]),
                    reads=[("pB", 0)], writes=[(k_rbc,)])
            P.ins("vector", lambda e, ec_=ec_, ph_=ph_: e.tensor_tensor(
                xn[ph_:ph_ + 64, ec_ * 1024:(ec_ + 1) * 1024],
                avs[0:64, :], rbc[0:64, :], ALU.mult),
                reads=[(k_avs,), (k_rbc,)], writes=[("xn", ec_)])
        return tail

    pending_tail = [None]
    load_kv_head(0, 0, 0)
    for kv in range(NKV):
        ki = vi = kv % 2
        if kv + 1 < NKV:
            load_kv_head(kv + 1, (kv + 1) % 2, (kv + 1) % 2)
        for g in range(4):
            h = kv * 4 + g
            pb_half = (h % 2) * 64
            ec = h // 2
            for lc in range(33):
                if lc < 32:
                    pS, sk = (pA, "pA") if lc % 2 == 0 else (pB, "pB")
                    prb, pk = pr[lc % 2], prk[lc % 2]
                    for th in range(2):
                        P.ins("tensor", lambda e, lc=lc, th=th, pS=pS, ki=ki,
                              pb_half=pb_half, ec=ec: e.matmul(
                            pS[:, th * 512:(th + 1) * 512],
                            kb[ki][pb_half:pb_half + 64, lc * 128:(lc + 1) * 128],
                            q_sb[pb_half:pb_half + 64,
                                 ec * 1024 + th * 512: ec * 1024 + (th + 1) * 512],
                            start=True, stop=True, skip_group_check=True),
                            reads=[(kbk[ki],), ("q_sb", ec)], writes=[(sk, th)])
                    P.ins("scalar", lambda e, pS=pS, prb=prb: e.activation(
                        prb[:], pS[:, :], AF.Exp, scale=np.float32(0.125)),
                        reads=[(sk, 0), (sk, 1)], writes=[(pk,)])
                if lc == 4 and pending_tail[0] is not None:
                    pending_tail[0]()
                    pending_tail[0] = None
                if lc >= 1:
                    av = lc - 1
                    prb, pk = pr[av % 2], prk[av % 2]
                    for th in range(2):
                        P.ins("tensor", lambda e, av=av, th=th, vi=vi, prb=prb:
                              e.matmul(
                            pC[0:65, th * 512:(th + 1) * 512],
                            vb[vi][:, (av // 8) * 520 + (av % 8) * 65:
                                   (av // 8) * 520 + (av % 8) * 65 + 65],
                            prb[:, th * 512:(th + 1) * 512],
                            start=(av == 0), stop=(av == 31),
                            skip_group_check=True),
                            reads=[(vbk[vi],), (pk,), ("pC", th)],
                            writes=[("pC", th)])
            P.ins("scalar", lambda e: e.copy(avs[0:65, :], pC[0:65, :]),
                  reads=list(C01), writes=[(k_avs,)])
            pending_tail[0] = make_tail(ec, pb_half)
    pending_tail[0]()
    pending_tail[0] = None

    cur, ck = wq_b, "wq_b"
    nxt, nk = wq_b2, "wq_b2"
    load_att_chunk(cur, ck, 24)
    for ec in range(16):
        if ec + 1 < 16:
            load_att_chunk(nxt, nk, 24 + ec + 1)
        pW, wk = (pA, "pA") if ec % 2 == 0 else (pB, "pB")
        for hcc in range(HC):
            for th in range(2):
                P.ins("tensor", lambda e, th=th, hcc=hcc, cur=cur, pW=pW: e.matmul(
                    pW[:, th * 512:(th + 1) * 512],
                    cur[:, hcc * 128:(hcc + 1) * 128],
                    xn[:, hcc * 1024 + th * 512: hcc * 1024 + (th + 1) * 512],
                    start=(hcc == 0), stop=(hcc == HC - 1), skip_group_check=True),
                    reads=[(ck,), ("xn", hcc), (wk, th)], writes=[(wk, th)])
        P.ins("vector", lambda e, ec=ec, pW=pW: e.tensor_tensor(
            x_sb[:, ec * 1024:(ec + 1) * 1024], pW[:, :],
            x_sb[:, ec * 1024:(ec + 1) * 1024], ALU.add),
            reads=[(wk, 0), (wk, 1), ("x_sb", ec)], writes=[("x_sb", ec)])
        cur, ck, nxt, nk = nxt, nk, cur, ck
    P.barrier()
    ctxC2.close()
    ctxC.close()

    ctxD = ExitStack()
    sbD = lambda n, s, d: ctxD.enter_context(nc.sbuf_tensor(n, s, d))
    wg2 = sbD("wg2", [128, HC * 512], BF16)
    wu2 = sbD("wu2", [128, HC * 512], BF16)
    wd2 = sbD("wd2", [128, HC * 512], BF16)
    int2 = sbD("int2", [128, 4 * 1024], BF16)
    gl2 = (sbD("glg2", [128, 1024], F32), sbD("gla2", [128, 1024], F32),
           sbD("glb2", [128, 1024], F32), sbD("glc2", [128, 1024], F32))

    emit_ln()
    emit_ffn(wg2, wu2, wd2, int2, gl2,
             ("wg2", "wu2", "wd2", "int2", "glg2", "gla2", "glb2", "glc2"))
    P.barrier()
    ctxD.close()

    ctxE = ExitStack()
    os_ = [ctxE.enter_context(nc.sbuf_tensor("out_stage_a", [128, D], F32)),
           ctxE.enter_context(nc.sbuf_tensor("out_stage_b", [128, D], F32))]
    osk = ["out_stage_a", "out_stage_b"]
    n_to = [0]
    for tc in range(8):
        buf, bk = os_[tc % 2], osk[tc % 2]
        for hc in range(HC):
            pT, pk = (pA, "pA") if n_to[0] % 2 == 0 else (pB, "pB")
            P.ins("tensor", lambda e, tc=tc, hc=hc, pT=pT: e.transpose(
                pT[:, 0:128],
                x_sb[:, hc * 1024 + tc * 128: hc * 1024 + tc * 128 + 128],
                c_id_f[:]),
                reads=[("x_sb", hc), ("c_id_f",)], writes=[(pk, 0)])
            P.ins("vector", lambda e, hc=hc, buf=buf, pT=pT: e.tensor_copy(
                buf[:, hc * 128: hc * 128 + 128], pT[:, 0:128]),
                reads=[(pk, 0)], writes=[(bk,)])
            n_to[0] += 1
        P.ins("gpsimd", lambda e, tc=tc, buf=buf: e.dma_start(
            out.ap()[tc * 128:(tc + 1) * 128, :], buf[:]),
            reads=[(bk,)], writes=[("out",)], stream="out")
    P.final_wait("gpsimd", "out")
    ctxE.close()

    block = ctx.enter_context(nc.Block())

    @block.sync
    def _(sync):
        P.run("sync", sync)

    @block.scalar
    def _(scalar):
        P.run("scalar", scalar)

    @block.vector
    def _(vector):
        P.run("vector", vector)

    @block.tensor
    def _(tensor):
        P.run("tensor", tensor)

    @block.gpsimd
    def _(gpsimd):
        P.run("gpsimd", gpsimd)

    ctx.close()
    return nc


def prepare_in_maps(L0_attn_x, L0_attn_Wq, L0_attn_Wk, L0_attn_Wv, L0_attn_Wo,
                    L0_ffn_gate_W_packed, L0_ffn_gate_scales,
                    L0_ffn_up_W_packed, L0_ffn_up_scales,
                    L0_ffn_down_W_packed, L0_ffn_down_scales):
    x = np.ascontiguousarray(np.asarray(L0_attn_x, np.float32).reshape(8192, D))
    Wq = np.asarray(L0_attn_Wq, np.float32)
    Wk = np.asarray(L0_attn_Wk, np.float32)
    Wv = np.asarray(L0_attn_Wv, np.float32)
    Wo = np.asarray(L0_attn_Wo, np.float32)
    gpk = np.ascontiguousarray(np.asarray(L0_ffn_gate_W_packed, np.int32))
    gscn = np.ascontiguousarray(np.asarray(L0_ffn_gate_scales, np.float32))
    upkk = np.ascontiguousarray(np.asarray(L0_ffn_up_W_packed, np.int32))
    uscn = np.ascontiguousarray(np.asarray(L0_ffn_up_scales, np.float32))
    dpk = np.asarray(L0_ffn_down_W_packed, np.int32)
    dscn = np.asarray(L0_ffn_down_scales, np.float32)

    aw_full = np.ascontiguousarray(np.concatenate([Wq, Wk, Wv, Wo], axis=0))

    half = 32
    inv_freq = 1.0 / (10000.0 ** (np.arange(half, dtype=np.float64) / half))

    in_maps = []
    for c in range(NCORES):
        pos = (c % 4) * TOK + np.arange(TOK, dtype=np.float64)
        ang = pos[None, :] * inv_freq[:, None]
        cs = np.cos(ang).astype(np.float32)
        sn = np.sin(ang).astype(np.float32)
        in_maps.append(dict(
            x_in=x[c * TOK:(c + 1) * TOK],
            gp=gpk[c * 2048:(c + 1) * 2048],
            gsc=gscn[c * 2048:(c + 1) * 2048],
            up_p=upkk[c * 2048:(c + 1) * 2048],
            usc=uscn[c * 2048:(c + 1) * 2048],
            dp=np.ascontiguousarray(dpk[:, c * 1024:(c + 1) * 1024]),
            dsc=np.ascontiguousarray(dscn[:, c * 64:(c + 1) * 64]),
            aw=np.ascontiguousarray(aw_full[c * 640:(c + 1) * 640]),
            cosq=np.ascontiguousarray(np.concatenate([cs, cs, cs, cs], axis=0)),
            sinq=np.ascontiguousarray(np.concatenate([sn, sn, sn, sn], axis=0)),
        ))
    return in_maps


def kernel(**inputs):
    in_maps = prepare_in_maps(**inputs)
    nc = build_program()
    res = run_bass_kernel_spmd(nc, in_maps, core_ids=list(range(NCORES)))
    outs = [np.asarray(res.results[c]["out"], np.float32) for c in range(NCORES)]
    return np.concatenate(outs, axis=0).reshape(2, SEQ, D)
